# version 03
# baseline (speedup 1.0000x reference)
import contextlib
import numpy as np
import concourse.bass as bass
import concourse.mybir as mybir
from concourse.bass_utils import run_bass_kernel_spmd

F32 = mybir.dt.float32
BF16 = mybir.dt.bfloat16
OP = mybir.AluOpType
AX = mybir.AxisListType.X
AF = mybir.ActivationFunctionType

NCORES = 8
B, S, H = 4, 8192, 1024
SC = S // NCORES
T = B * SC
NT = T // 128
TPB = SC // 128
NH, HD = 16, 64
FF = 2816
NFC = H // 128
EPS_OUT = 1e-5
EPS_IN = 1e-12
EPS_ATTN = 1e-6

M6 = [m / 6.0 for m in (0.25, 0.75, 1.25, 1.75, 2.5, 3.5, 5.0)]

ENGINES = ("sync", "gpsimd", "tensor", "vector", "scalar")

_N_SUB = 4
_DEBUG_TAPS = False
_STOP_AFTER = None


class _Buf:
    __slots__ = ("last_write", "readers", "name")

    def __init__(self, name=""):
        self.last_write = None
        self.readers = []
        self.name = name


class _Task:
    __slots__ = ("engine", "fns", "deps", "fdeps", "is_dma", "is_coll",
                 "needs_inc", "val", "idx", "key", "gate")

    def __init__(self, engine, fns, deps, fdeps, is_dma, is_coll, idx):
        self.engine = engine
        self.fns = fns
        self.deps = deps
        self.fdeps = fdeps
        self.is_dma = is_dma
        self.is_coll = is_coll
        self.needs_inc = False
        self.val = 0
        self.idx = idx


class Sched:

    def __init__(self):
        self.tasks = []
        self.barrier = {}

    def add(self, engine, fns, reads=(), writes=(), after=(), force=(),
            is_dma=False, is_coll=False):
        if callable(fns):
            fns = [fns]
        deps = set()
        fdeps = set()
        for b in force:
            if b.last_write is not None:
                fdeps.add(b.last_write)
        for b in list(reads) + list(force):
            if b.last_write is not None:
                deps.add(b.last_write)
        for b in writes:
            if b.last_write is not None:
                deps.add(b.last_write)
            deps.update(b.readers)
        for t in after:
            if t is not None:
                deps.add(t)
        for t in self.barrier.values():
            deps.add(t)
        t = _Task(engine, fns, deps, fdeps, is_dma, is_coll, len(self.tasks))
        self.tasks.append(t)
        for b in list(reads) + list(force):
            b.readers.append(t)
        for b in writes:
            b.last_write = t
            b.readers = []
        return t

    def phase_barrier(self):
        last = {}
        for t in self.tasks:
            last[t.engine] = t
        self.barrier = last

    NLANES = 8

    def build(self, nc, stack):
        for t in self.tasks:
            if t.is_dma or t.is_coll:
                t.needs_inc = True
            for d in t.deps:
                assert d.idx < t.idx
                if not (d.engine == t.engine == "tensor"
                        and not (d.is_dma or d.is_coll)):
                    d.needs_inc = True
            for d in t.fdeps:
                d.needs_inc = True
        lane_ctr = {e: 0 for e in ENGINES}
        lane_cum = {e: [0] * self.NLANES for e in ENGINES}
        counts = {e: 0 for e in ENGINES}
        n_coll = 0
        for t in self.tasks:
            if t.is_dma:
                lane = lane_ctr[t.engine] % self.NLANES
                lane_ctr[t.engine] += 1
                t.key = (t.engine, lane)
                t.gate = lane_cum[t.engine][lane]
                lane_cum[t.engine][lane] += 16 * len(t.fns)
                t.val = lane_cum[t.engine][lane]
            elif t.is_coll:
                t.key = ("coll", 0)
                n_coll += 1
                t.val = n_coll
            else:
                t.key = (t.engine, -1)
                if t.needs_inc:
                    counts[t.engine] += 1
                t.val = counts[t.engine]
        sems = {}
        for e in ENGINES:
            sems[(e, -1)] = stack.enter_context(nc.semaphore(f"s_{e}"))
            for l in range(self.NLANES):
                sems[(e, l)] = stack.enter_context(nc.semaphore(f"s_{e}_l{l}"))
        sems[("coll", 0)] = stack.enter_context(nc.semaphore("s_coll"))
        per = {e: [t for t in self.tasks if t.engine == e] for e in ENGINES}
        hwm = {e: {} for e in ENGINES}

        def runner(ename):
            def body(eng):
                hw = hwm[ename]
                for t in per[ename]:
                    need = {}
                    for d in t.deps:
                        if (d.engine == ename == "tensor"
                                and not (d.is_dma or d.is_coll)
                                and d not in t.fdeps):
                            continue
                        k = d.key
                        need[k] = max(need.get(k, 0), d.val)
                    if t.is_dma and t.gate > 0:
                        k = t.key
                        need[k] = max(need.get(k, 0), t.gate)
                    for k in sorted(need):
                        v = need[k]
                        if v > hw.get(k, 0):
                            eng.wait_ge(sems[k], v)
                            hw[k] = v
                    insts = [fn(eng) for fn in t.fns]
                    if t.needs_inc:
                        if t.is_dma:
                            for i in insts:
                                i.then_inc(sems[t.key], 16)
                        elif t.is_coll:
                            insts[-1].then_inc(sems[t.key], 1)
                        else:
                            insts[-1].then_inc(sems[t.key], 1)
            return body

        with nc.Block() as block:
            block.sync(runner("sync"))
            block.gpsimd(runner("gpsimd"))
            block.tensor(runner("tensor"))
            block.vector(runner("vector"))
            block.scalar(runner("scalar"))


def build_nc(n_sub=4):
    nc = bass.Bass("TRN2", target_bir_lowering=False)
    sch = Sched()

    d_x = nc.dram_tensor("x", [T, H], F32, kind="ExternalInput").ap()
    d_y = nc.dram_tensor("y", [T, H], F32, kind="ExternalOutput").ap()
    d_ident = nc.dram_tensor("ident", [128, 128], F32, kind="ExternalInput").ap()
    d_wqT = nc.dram_tensor("wqT", [H, H], F32, kind="ExternalInput").ap()
    d_wkT = nc.dram_tensor("wkT", [H, H], F32, kind="ExternalInput").ap()
    d_wvT = nc.dram_tensor("wvT", [H, H], F32, kind="ExternalInput").ap()
    d_wdT = nc.dram_tensor("wdT", [H, H], F32, kind="ExternalInput").ap()
    d_bq = nc.dram_tensor("bq", [128, NFC], F32, kind="ExternalInput").ap()
    d_bk_bc = nc.dram_tensor("bk_bc", [128, H], F32, kind="ExternalInput").ap()
    d_bv_bc = nc.dram_tensor("bv_bc", [128, H], F32, kind="ExternalInput").ap()
    d_bd_bc = nc.dram_tensor("bd_bc", [128, H], F32, kind="ExternalInput").ap()
    d_ga_bc = nc.dram_tensor("ga_bc", [128, H], F32, kind="ExternalInput").ap()
    d_be_bc = nc.dram_tensor("be_bc", [128, H], F32, kind="ExternalInput").ap()
    d_wkb = nc.dram_tensor("wkb", [H, H], F32, kind="ExternalInput").ap()
    d_wvb = nc.dram_tensor("wvb", [H, H], F32, kind="ExternalInput").ap()
    d_wob = nc.dram_tensor("wob", [H, H], F32, kind="ExternalInput").ap()
    d_mk = nc.dram_tensor("mk", [128, NFC], F32, kind="ExternalInput").ap()
    d_mv = nc.dram_tensor("mv", [128, NFC], F32, kind="ExternalInput").ap()
    d_td = nc.dram_tensor("td", [128, NFC], F32, kind="ExternalInput").ap()
    d_tf = nc.dram_tensor("tf", [128, NFC], F32, kind="ExternalInput").ap()
    d_wg = nc.dram_tensor("wg", [FF, H], F32, kind="ExternalInput").ap()
    d_wu = nc.dram_tensor("wu", [FF, H], F32, kind="ExternalInput").ap()
    d_wd = nc.dram_tensor("wd", [H, FF], F32, kind="ExternalInput").ap()
    d_wgT = nc.dram_tensor("wgT_q", [H, FF], F32).ap()
    d_wuT = nc.dram_tensor("wuT_q", [H, FF], F32).ap()
    d_wdTq = nc.dram_tensor("wdT_q", [FF, H], F32).ap()
    d_kv_in = nc.dram_tensor("kv_in", [64, B * NH * 65], F32).ap()
    d_kv_out = nc.dram_tensor("kv_out", [64, B * NH * 65], F32).ap()
    if _DEBUG_TAPS:
        d_tap_kvi = nc.dram_tensor("tap_kvi", [64, B * NH * 65], F32, kind="ExternalOutput").ap()
        d_tap_kvo = nc.dram_tensor("tap_kvo", [64, B * NH * 65], F32, kind="ExternalOutput").ap()
        d_tap_k = nc.dram_tensor("tap_k", [128, H], F32, kind="ExternalOutput").ap()
        d_tap_v = nc.dram_tensor("tap_v", [128, NH * 65], F32, kind="ExternalOutput").ap()
        d_tap_q = nc.dram_tensor("tap_q", [128, H], F32, kind="ExternalOutput").ap()
        d_tap_ctx = nc.dram_tensor("tap_ctx", [128, H], F32, kind="ExternalOutput").ap()
        d_tap_z = nc.dram_tensor("tap_z", [128, NH], F32, kind="ExternalOutput").ap()
        d_tap_hs = nc.dram_tensor("tap_hs", [128, H], F32, kind="ExternalOutput").ap()
        d_tap_kv2 = nc.dram_tensor("tap_kv2", [128, B * 8 * 130], F32, kind="ExternalOutput").ap()
        d_tap_xln = nc.dram_tensor("tap_xln", [128, H], F32, kind="ExternalOutput").ap()
        d_tap_xlnT = nc.dram_tensor("tap_xlnT", [128, H], F32, kind="ExternalOutput").ap()
        d_tap_st = nc.dram_tensor("tap_st", [128, 2 * NT], F32, kind="ExternalOutput").ap()
        d_tap_ps = nc.dram_tensor("tap_ps", [128, 512], F32, kind="ExternalOutput").ap()
        d_tap_wb = nc.dram_tensor("tap_wb", [128, 512], F32, kind="ExternalOutput").ap()

    def sb(name, p, f, dt=F32):
        t = nc.alloc_sbuf_tensor(name, [p, f], dt)
        return t.ap(), _Buf(name)

    xres, _ = sb("xres", 128, NT * H)
    b_xt = [_Buf(f"x{i}") for i in range(NT)]
    ident, b_ident = sb("ident_t", 128, 128)
    bq_s, b_bq = sb("bq_s", 128, NFC)
    bk_bc, b_bk = sb("bk_t", 128, H)
    bv_bc, b_bv = sb("bv_t", 128, H)
    bd_bc, b_bd = sb("bd_t", 128, H)
    ga_bc, b_ga = sb("ga_t", 128, H)
    be_bc, b_be = sb("be_t", 128, H)
    mk_s, b_mk = sb("mk_s", 128, NFC)
    mv_s, b_mv = sb("mv_s", 128, NFC)
    td_s, b_td = sb("td_s", 128, NFC)
    tf_s, b_tf = sb("tf_s", 128, NFC)
    tdtf_s, b_tdtf = sb("tdtf_s", 128, NFC)
    lnmu, b_lnmu = sb("lnmu", 128, NT)
    lnrs, b_lnrs = sb("lnrs", 128, NT)
    lnsum, b_lnsum = sb("lnsum", 128, NT)
    lnssq, b_lnssq = sb("lnssq", 128, NT)
    lnsc1, b_lnsc1 = sb("lnsc1", 128, NT)
    lnsc2, b_lnsc2 = sb("lnsc2", 128, NT)
    epso, b_epso = sb("epso", 128, 1)
    epsi, b_epsi = sb("epsi", 128, 1)
    stat, b_stat = sb("stat", 128, 8)

    POOLF = 13440
    pool_t = nc.alloc_sbuf_tensor("pool", [128, POOLF], F32)
    pool = pool_t.ap()
    _pool_off = [0]

    def carve(name, f):
        o = _pool_off[0]
        assert o + f <= POOLF, f"pool overflow at {name}: {o}+{f}"
        _pool_off[0] = o + f
        return pool[:, o:o + f], _Buf(name)

    def pool_reset():
        _pool_off[0] = 0
        sch.phase_barrier()

    stack = contextlib.ExitStack()
    psum = [stack.enter_context(nc.psum_tensor(f"ps{i}", [128, 512], F32))
            for i in range(8)]
    b_ps = [_Buf(f"ps{i}") for i in range(8)]

    def dma(engine, out_ap, in_ap, reads=(), writes=(), after=()):
        return sch.add(engine, lambda e: e.dma_start(out=out_ap, in_=in_ap),
                       reads=reads, writes=writes, after=after, is_dma=True)

    def vop(fn, reads=(), writes=(), after=(), force=(), eng="vector"):
        return sch.add(eng, fn, reads=reads, writes=writes, after=after,
                       force=force)

    def sop(fn, reads=(), writes=(), after=(), force=()):
        return sch.add("scalar", fn, reads=reads, writes=writes, after=after,
                       force=force)

    def top(fn, reads=(), writes=(), after=(), force=()):
        return sch.add("tensor", fn, reads=reads, writes=writes, after=after,
                       force=force)

    def tile_x(i):
        return xres[:, H * i: H * (i + 1)]

    dma("sync", ident, d_ident[:, :], writes=[b_ident])
    dma("sync", bq_s, d_bq[:, :], writes=[b_bq])
    dma("sync", bk_bc, d_bk_bc[:, :], writes=[b_bk])
    dma("sync", bv_bc, d_bv_bc[:, :], writes=[b_bv])
    dma("sync", bd_bc, d_bd_bc[:, :], writes=[b_bd])
    dma("sync", ga_bc, d_ga_bc[:, :], writes=[b_ga])
    dma("sync", be_bc, d_be_bc[:, :], writes=[b_be])
    dma("sync", mk_s, d_mk[:, :], writes=[b_mk])
    dma("sync", mv_s, d_mv[:, :], writes=[b_mv])
    dma("sync", td_s, d_td[:, :], writes=[b_td])
    dma("sync", tf_s, d_tf[:, :], writes=[b_tf])
    vop(lambda e: e.tensor_tensor(tdtf_s, td_s, tf_s, op=OP.add),
        reads=[b_td, b_tf], writes=[b_tdtf])
    vop(lambda e: e.memset(epso, EPS_OUT), writes=[b_epso])
    vop(lambda e: e.memset(epsi, EPS_IN), writes=[b_epsi])

    d_x4 = d_x.rearrange("(n p) h -> n p h", p=128)
    loads = []
    for i in range(NT):
        loads.append((lambda i=i: lambda e: e.dma_start(
            out=tile_x(i), in_=d_x4[i, :, :]))())
    sch.add("sync", loads, writes=b_xt, is_dma=True)

    def ln_stats_all(eps_ap, b_eps, sq_scratch, b_sq):
        for i in range(NT):
            vop((lambda i=i: lambda e: e.reduce_sum(
                lnsum[:, i:i + 1], tile_x(i), axis=AX))(),
                reads=[b_xt[i]], writes=[b_lnsum])
            sop((lambda i=i, sq=sq_scratch: lambda e: e.activation(
                sq, tile_x(i), AF.Square, accum_out=lnssq[:, i:i + 1]))(),
                reads=[b_xt[i]], force=[b_sq], writes=[b_sq, b_lnssq])
        vop(lambda e: e.tensor_scalar(lnmu, lnsum, 1.0 / H, 0.0,
                                      op0=OP.mult, op1=OP.add),
            force=[b_lnsum], writes=[b_lnmu])
        vop(lambda e: e.tensor_tensor(lnsc1, lnmu, lnmu, op=OP.mult),
            force=[b_lnmu], writes=[b_lnsc1])
        vop(lambda e: e.scalar_tensor_tensor(lnsc2, lnssq, 1.0 / H, lnsc1,
                                             op0=OP.mult, op1=OP.subtract),
            force=[b_lnssq, b_lnsc1], writes=[b_lnsc2])
        sop(lambda e: e.activation(lnsc1, lnsc2, AF.Sqrt, bias=eps_ap),
            force=[b_lnsc2], reads=[b_eps], writes=[b_lnsc1])
        vop(lambda e: e.reciprocal(lnrs, lnsc1), force=[b_lnsc1],
            writes=[b_lnrs])

    def ln_apply(i, src, b_src, dst, b_dst):
        vop((lambda i=i, src=src, dst=dst: lambda e: e.tensor_scalar(
            dst, src, lnmu[:, i:i + 1], lnrs[:, i:i + 1],
            op0=OP.subtract, op1=OP.mult))(),
            reads=list(b_src), force=[b_lnmu, b_lnrs], writes=[b_dst])

    def transpose8(src, b_src, dst, b_dst, nch=NFC):
        for c in range(nch):
            top((lambda c=c: lambda e: e.transpose(
                psum[0][:, 0:128], src[:, 128 * c:128 * (c + 1)], ident))(),
                reads=[b_src, b_ident], writes=[b_ps[0]])
            vop((lambda c=c: lambda e: e.tensor_copy(
                dst[:, 128 * c:128 * c + 128], psum[0][:, 0:128]))(),
                reads=[b_ps[0]], writes=[b_dst])

    xln, b_xln = carve("xln", H)
    sqs, b_sqs = carve("sqs", H)
    xlnT, b_xlnT = carve("xlnT", H)
    k_sb, b_ksb = carve("k_sb", H)
    v65, b_v65 = carve("v65", NH * 65)
    kvst, b_kvst = carve("kvst", NH * 65)
    wba, b_wba = carve("wba", 512)
    wbb, b_wbb = carve("wbb", 512)
    elu_t, b_elu = carve("elu_t", H)
    wbuf = [wba, wbb]
    b_wbuf = [b_wba, b_wbb]
    kvacc, b_kvacc = carve("kvacc", NH * 65)
    v65_3 = v65.rearrange("p (h c) -> p h c", c=65)

    vop(lambda e: e.memset(v65_3[:, :, 64:65], 1.0), writes=[b_v65])

    KVBANK = {**{h: (1, h) for h in range(7)},
              **{h: (6, h - 7) for h in range(7, 14)},
              **{h: (7, h - 14) for h in range(14, 16)}}

    kv_dmas = []
    ln_stats_all(epso, b_epso, sqs, b_sqs)
    for b in range(B) if _STOP_AFTER != "k0" else range(1):
        for jt in range(TPB) if _STOP_AFTER != "k0" else range(1):
            i = TPB * b + jt
            ln_apply(i, tile_x(i), [b_xt[i]], xln, b_xln)
            transpose8(xln, b_xln, xlnT, b_xlnT)
            for wdram, bias_bc, b_bias, kbank, is_k in (
                    (d_wkT, bk_bc, b_bk, 2, True), (d_wvT, bv_bc, b_bv, 4, False)):
                for half in range(2):
                    pb = b_ps[kbank + half]
                    ps_ap = psum[kbank + half]
                    for fi in range(NFC):
                        wb, bwb = wbuf[fi % 2], b_wbuf[fi % 2]
                        dma("sync", wb,
                            wdram[128 * fi:128 * (fi + 1), 512 * half:512 * (half + 1)],
                            writes=[bwb])
                        top((lambda fi=fi, wb=wb, ps_ap=ps_ap: lambda e: e.matmul(
                            ps_ap[:, :], xlnT[:, 128 * fi:128 * (fi + 1)], wb,
                            start=(fi == 0), stop=(fi == NFC - 1)))(),
                            reads=[b_xlnT, bwb], writes=[pb])
                    if _DEBUG_TAPS and i == 0 and is_k and half == 0:
                        tps = vop((lambda ps_ap=ps_ap: lambda e: e.tensor_copy(
                            elu_t[:, 0:512], ps_ap[:, :]))(), reads=[pb], writes=[b_elu])
                        dma("gpsimd", d_tap_ps[:, :], elu_t[:, 0:512], reads=[b_elu])
                        dma("gpsimd", d_tap_wb[:, :], wbuf[7 % 2], reads=[b_wbuf[7 % 2]])
                    if is_k:
                        vop((lambda ps_ap=ps_ap, half=half, bias_bc=bias_bc:
                             lambda e: e.scalar_tensor_tensor(
                                 k_sb[:, 512 * half:512 * (half + 1)], ps_ap[:, :],
                                 0.0, bias_bc[:, 512 * half:512 * (half + 1)],
                                 op0=OP.bypass, op1=OP.add))(),
                            reads=[pb, b_bias], writes=[b_ksb])
                        vop((lambda half=half: lambda e: e.tensor_scalar(
                            elu_t[:, 512 * half:512 * (half + 1)],
                            k_sb[:, 512 * half:512 * (half + 1)], 0.0, 0.0,
                            op0=OP.min, op1=OP.add))(),
                            reads=[b_ksb], writes=[b_elu])
                        sop((lambda half=half: lambda e: e.activation(
                            elu_t[:, 512 * half:512 * (half + 1)],
                            elu_t[:, 512 * half:512 * (half + 1)], AF.Exp))(),
                            reads=[b_elu], writes=[b_elu])
                        vop((lambda half=half: lambda e: e.scalar_tensor_tensor(
                            k_sb[:, 512 * half:512 * (half + 1)],
                            k_sb[:, 512 * half:512 * (half + 1)], 0.0,
                            elu_t[:, 512 * half:512 * (half + 1)],
                            op0=OP.max, op1=OP.add))(),
                            reads=[b_elu], writes=[b_ksb])
                    else:
                        vop((lambda ps_ap=ps_ap, half=half, bias_bc=bias_bc:
                             lambda e: e.scalar_tensor_tensor(
                                 v65_3[:, 8 * half:8 * (half + 1), 0:64],
                                 ps_ap.rearrange("p (h c) -> p h c", c=64), 0.0,
                                 bias_bc.rearrange("p (h c) -> p h c", c=64)[:, 8 * half:8 * (half + 1), :],
                                 op0=OP.bypass, op1=OP.add))(),
                            reads=[pb, b_bias], writes=[b_v65])
            if _DEBUG_TAPS and i == 0:
                dma("gpsimd", d_tap_k[:, :], k_sb, reads=[b_ksb])
                dma("gpsimd", d_tap_v[:, :], v65, reads=[b_v65])
                dma("gpsimd", d_tap_xln[:, :], xln, reads=[b_xln])
                dma("gpsimd", d_tap_xlnT[:, :], xlnT, reads=[b_xlnT])
                dma("gpsimd", d_tap_st[:, 0:NT], lnmu, reads=[b_lnmu])
                dma("gpsimd", d_tap_st[:, NT:2 * NT], lnrs, reads=[b_lnrs])
            if _STOP_AFTER == "k0" and i == 0:
                break
            for h in range(NH):
                bank, slot = KVBANK[h]
                top((lambda h=h, bank=bank, slot=slot: lambda e: e.matmul(
                    psum[bank][0:64, 65 * slot:65 * (slot + 1)],
                    k_sb[:, 64 * h:64 * (h + 1)], v65[:, 65 * h:65 * (h + 1)],
                    start=True, stop=True))(),
                    reads=[b_ksb, b_v65], writes=[b_ps[bank]])
            for bank, off, nh_in in ((1, 0, 7), (6, 7, 7), (7, 14, 2)):
                if jt == 0:
                    vop((lambda bank=bank, off=off, nh_in=nh_in: lambda e: e.tensor_copy(
                        kvacc[0:64, 65 * off:65 * (off + nh_in)],
                        psum[bank][0:64, 0:65 * nh_in]))(),
                        reads=[b_ps[bank]], writes=[b_kvacc])
                else:
                    vop((lambda bank=bank, off=off, nh_in=nh_in: lambda e: e.tensor_tensor(
                        kvacc[0:64, 65 * off:65 * (off + nh_in)],
                        kvacc[0:64, 65 * off:65 * (off + nh_in)],
                        psum[bank][0:64, 0:65 * nh_in], op=OP.add))(),
                        reads=[b_ps[bank]], force=[b_kvacc], writes=[b_kvacc])
        if _STOP_AFTER == "k0":
            continue
        vop(lambda e: e.tensor_copy(kvst[0:64, :], kvacc[0:64, :]),
            force=[b_kvacc], reads=[], writes=[b_kvst])
        kv_dmas.append(dma("gpsimd", d_kv_in[:, 1040 * b:1040 * (b + 1)],
                           kvst[0:64, :], reads=[b_kvst]))

    if _DEBUG_TAPS:
        kvsnap, b_kvsnap = carve("kvsnap", NH * 65 * B)
        t_s1 = dma("gpsimd", kvsnap[0:64, :], d_kv_in[:, :], writes=[b_kvsnap],
                   after=kv_dmas)
        t_s2 = dma("gpsimd", d_tap_kvi[:, :], kvsnap[0:64, :], reads=[b_kvsnap])
        kv_dmas = kv_dmas + [t_s2]
    if _STOP_AFTER == "phA":
        sch.add("gpsimd", lambda e: e.engine_nop(), after=kv_dmas)
        sch.build(nc, stack)
        return nc
    if _STOP_AFTER == "k0":
        st0 = dma("sync", d_y.rearrange("(n p) h -> n p h", p=128)[0, :, :], k_sb,
                  reads=[b_ksb])
        st1 = dma("sync", d_y.rearrange("(n p) h -> n p h", p=128)[1, :, :],
                  v65[:, 0:H], reads=[b_v65])
        st2 = dma("sync", d_y.rearrange("(n p) h -> n p h", p=128)[2, :, :],
                  xlnT, reads=[b_xlnT])
        sch.add("gpsimd", lambda e: e.engine_nop(), after=[st0, st1, st2])
        sch.build(nc, stack)
        return nc
    t_ar = sch.add("gpsimd", lambda e: e.collective_compute(
        "AllReduce", OP.add, replica_groups=[list(range(NCORES))],
        ins=[d_kv_in[:, :].opt()], outs=[d_kv_out[:, :].opt()]),
        after=kv_dmas, is_coll=True)

    if _STOP_AFTER == "ar":
        sch.add("gpsimd", lambda e: e.engine_nop(), after=[t_ar])
        sch.build(nc, stack)
        return nc
    pool_reset()
    p2_xln, p2_bxln = carve("xlnB", H)
    p2_xlnT, p2_bxlnT = carve("xlnTB", H)
    qfm, b_qfm = carve("qfm", H)
    kv2, b_kv2 = carve("kv2", B * 8 * 130)
    kvst2, b_kvst2 = carve("kvst2", NH * 65)
    ctx_sb, b_ctx = carve("ctx_sb", H)
    ctxT, b_ctxT = carve("ctxT", H)
    zden, b_zden = carve("zden", NH)
    elu2, b_elu2 = carve("elu2", H)
    hs_sb, b_hs = carve("hs_sb", H)
    wba2, b_wba2 = carve("wba2", 512)
    wbb2, b_wbb2 = carve("wbb2", 512)
    wbuf2 = [wba2, wbb2]
    b_wbuf2 = [b_wba2, b_wbb2]

    if _STOP_AFTER == "kv2m":
        sch.add("gpsimd", lambda e: e.engine_nop(), after=[t_ar])
        sch.build(nc, stack)
        return nc
    if _STOP_AFTER == "kv2z":
        vop(lambda e: e.memset(pool[:, 12500:12600], 0.0), writes=[b_kv2], after=[t_ar])
        sch.add("gpsimd", lambda e: e.engine_nop(), after=[t_ar, b_kv2.last_write])
        sch.build(nc, stack)
        return nc
    vop(lambda e: e.memset(kv2, 0.0), writes=[b_kv2], after=[t_ar])
    if _STOP_AFTER == "kv2a":
        sch.add("gpsimd", lambda e: e.engine_nop(), after=[t_ar, b_kv2.last_write])
        sch.build(nc, stack)
        return nc
    for b in range(B):
        dma("gpsimd", kvst2[0:64, :], d_kv_out[:, 1040 * b:1040 * (b + 1)],
            writes=[b_kvst2], after=[t_ar])

        def mk_build(b=b):
            def f(e):
                last = None
                for p in range(8):
                    base = 1040 * b + 130 * p
                    last = e.tensor_copy(kv2[0:64, base:base + 65],
                                         kvst2[0:64, 130 * p:130 * p + 65])
                    last = e.tensor_copy(kv2[64:128, base + 65:base + 130],
                                         kvst2[0:64, 130 * p + 65:130 * (p + 1)])
                return last
            return f
        vop(mk_build(), reads=[b_kvst2], writes=[b_kv2])

    def eps_f(e):
        v3 = kv2.rearrange("p (x c) -> p x c", c=130)
        e.tensor_scalar(v3[0:64, :, 64:65], v3[0:64, :, 64:65], EPS_ATTN, 0.0,
                        op0=OP.add, op1=OP.add)
        return e.tensor_scalar(v3[64:128, :, 129:130], v3[64:128, :, 129:130],
                               EPS_ATTN, 0.0, op0=OP.add, op1=OP.add)
    vop(eps_f, force=[b_kv2], writes=[b_kv2])
    if _STOP_AFTER == "kv2":
        sch.build(nc, stack)
        return nc

    CTXBANK = [(4 + p // 3, p % 3) for p in range(8)]

    for i in range(NT):
        b = i // TPB
        ln_apply(i, tile_x(i), [b_xt[i]], p2_xln, p2_bxln)
        transpose8(p2_xln, p2_bxln, p2_xlnT, p2_bxlnT)
        for c in range(NFC):
            pb = b_ps[2 + c % 2]
            ps_ap = psum[2 + c % 2]
            for fi in range(NFC):
                wb, bwb = wbuf2[fi % 2], b_wbuf2[fi % 2]
                dma("sync", wb[:, 0:128],
                    d_wqT[128 * fi:128 * (fi + 1), 128 * c:128 * (c + 1)],
                    writes=[bwb])
                top((lambda fi=fi, wb=wb, ps_ap=ps_ap: lambda e: e.matmul(
                    ps_ap[:, 0:128], wb[:, 0:128],
                    p2_xlnT[:, 128 * fi:128 * (fi + 1)],
                    start=(fi == 0), stop=(fi == NFC - 1)))(),
                    reads=[p2_bxlnT, bwb], writes=[pb])
            vop((lambda c=c, ps_ap=ps_ap: lambda e: e.tensor_scalar(
                elu2[:, 128 * c:128 * (c + 1)], ps_ap[:, 0:128],
                bq_s[:, c:c + 1], 0.0, op0=OP.add, op1=OP.min))(),
                reads=[pb, b_bq], writes=[b_elu2])
            sop((lambda c=c: lambda e: e.activation(
                elu2[:, 128 * c:128 * (c + 1)], elu2[:, 128 * c:128 * (c + 1)],
                AF.Exp))(), reads=[b_elu2], writes=[b_elu2])
            sop((lambda c=c, ps_ap=ps_ap: lambda e: e.activation(
                qfm[:, 128 * c:128 * (c + 1)], ps_ap[:, 0:128], AF.Relu,
                bias=bq_s[:, c:c + 1]))(), reads=[pb, b_bq], writes=[b_qfm])
            vop((lambda c=c: lambda e: e.tensor_tensor(
                qfm[:, 128 * c:128 * (c + 1)], qfm[:, 128 * c:128 * (c + 1)],
                elu2[:, 128 * c:128 * (c + 1)], op=OP.add))(),
                reads=[b_elu2], writes=[b_qfm])
        for p in range(8):
            bank, slot = CTXBANK[p]
            top((lambda p=p, b=b, bank=bank, slot=slot: lambda e: e.matmul(
                psum[bank][:, 130 * slot:130 * (slot + 1)],
                qfm[:, 128 * p:128 * (p + 1)],
                kv2[:, 1040 * b + 130 * p:1040 * b + 130 * (p + 1)],
                start=True, stop=True))(),
                reads=[b_qfm, b_kv2], writes=[b_ps[bank]])

        def gden(e):
            last = None
            for p in range(8):
                bank, slot = CTXBANK[p]
                last = e.tensor_copy(zden[:, 2 * p:2 * p + 1],
                                     psum[bank][:, 130 * slot + 64:130 * slot + 65])
                last = e.tensor_copy(zden[:, 2 * p + 1:2 * p + 2],
                                     psum[bank][:, 130 * slot + 129:130 * slot + 130])
            return last
        vop(gden, reads=[b_ps[4], b_ps[5], b_ps[6]], writes=[b_zden])
        vop(lambda e: e.reciprocal(zden, zden), force=[b_zden], writes=[b_zden])

        def gctx(e):
            last = None
            for p in range(8):
                bank, slot = CTXBANK[p]
                last = e.tensor_scalar(ctx_sb[:, 128 * p:128 * p + 64],
                                       psum[bank][:, 130 * slot:130 * slot + 64],
                                       zden[:, 2 * p:2 * p + 1], 0.0,
                                       op0=OP.mult, op1=OP.add)
                last = e.tensor_scalar(ctx_sb[:, 128 * p + 64:128 * (p + 1)],
                                       psum[bank][:, 130 * slot + 65:130 * slot + 129],
                                       zden[:, 2 * p + 1:2 * p + 2], 0.0,
                                       op0=OP.mult, op1=OP.add)
            return last
        vop(gctx, reads=[b_ps[4], b_ps[5], b_ps[6]], force=[b_zden], writes=[b_ctx])
        if _DEBUG_TAPS and i == 0:
            dma("gpsimd", d_tap_q[:, :], qfm, reads=[b_qfm])
            dma("gpsimd", d_tap_ctx[:, :], ctx_sb, reads=[b_ctx])
            dma("gpsimd", d_tap_z[:, :], zden, reads=[b_zden])
            dma("gpsimd", d_tap_kv2[:, :], kv2, reads=[b_kv2])
        transpose8(ctx_sb, b_ctx, ctxT, b_ctxT)
        for half in range(2):
            pb = b_ps[2 + half]
            ps_ap = psum[2 + half]
            for fi in range(NFC):
                wb, bwb = wbuf2[fi % 2], b_wbuf2[fi % 2]
                dma("sync", wb,
                    d_wdT[128 * fi:128 * (fi + 1), 512 * half:512 * (half + 1)],
                    writes=[bwb])
                top((lambda fi=fi, wb=wb, ps_ap=ps_ap: lambda e: e.matmul(
                    ps_ap[:, :], ctxT[:, 128 * fi:128 * (fi + 1)], wb,
                    start=(fi == 0), stop=(fi == NFC - 1)))(),
                    reads=[b_ctxT, bwb], writes=[pb])
            vop((lambda half=half, ps_ap=ps_ap: lambda e: e.scalar_tensor_tensor(
                hs_sb[:, 512 * half:512 * (half + 1)], ps_ap[:, :], 0.0,
                bd_bc[:, 512 * half:512 * (half + 1)],
                op0=OP.bypass, op1=OP.add))(),
                reads=[b_ps[2 + half], b_bd], writes=[b_hs])
        vop(lambda e: e.tensor_tensor(hs_sb, hs_sb, p2_xln, op=OP.add),
            reads=[p2_bxln], writes=[b_hs])
        if _DEBUG_TAPS and i == 0:
            dma("gpsimd", d_tap_hs[:, :], hs_sb, reads=[b_hs])
        vop(lambda e: e.reduce_sum(stat[:, 0:1], hs_sb, axis=AX),
            reads=[b_hs], writes=[b_stat])
        sop(lambda e: e.activation(elu2, hs_sb, AF.Square, accum_out=stat[:, 1:2]),
            reads=[b_hs], writes=[b_elu2, b_stat])
        vop(lambda e: e.tensor_scalar(stat[:, 5:6], stat[:, 0:1], 1.0 / H, 0.0,
                                      op0=OP.mult, op1=OP.add),
            force=[b_stat], writes=[b_stat])
        vop(lambda e: e.tensor_tensor(stat[:, 2:3], stat[:, 5:6], stat[:, 5:6],
                                      op=OP.mult),
            force=[b_stat], writes=[b_stat])
        vop(lambda e: e.scalar_tensor_tensor(stat[:, 3:4], stat[:, 1:2], 1.0 / H,
                                             stat[:, 2:3], op0=OP.mult,
                                             op1=OP.subtract),
            force=[b_stat], writes=[b_stat])
        sop(lambda e: e.activation(stat[:, 4:5], stat[:, 3:4], AF.Sqrt, bias=epsi),
            force=[b_stat], reads=[b_epsi], writes=[b_stat])
        vop(lambda e: e.reciprocal(stat[:, 6:7], stat[:, 4:5]),
            force=[b_stat], writes=[b_stat])
        vop(lambda e: e.tensor_scalar(hs_sb, hs_sb, stat[:, 5:6], stat[:, 6:7],
                                      op0=OP.subtract, op1=OP.mult),
            force=[b_stat], writes=[b_hs])
        vop(lambda e: e.tensor_tensor(hs_sb, hs_sb, ga_bc, op=OP.mult),
            reads=[b_ga], writes=[b_hs])
        vop(lambda e: e.tensor_tensor(hs_sb, hs_sb, be_bc, op=OP.add),
            reads=[b_be], writes=[b_hs])
        vop((lambda i=i: lambda e: e.tensor_tensor(
            tile_x(i), tile_x(i), hs_sb, op=OP.add))(),
            reads=[b_hs], writes=[b_xt[i]])

    def emit_quant(src, b_src, dst, b_dst, tA, b_tA, tB, b_tB, tW, b_tW,
                   bmv, b_bmv, D):
        NB = D // 16
        bm3 = bmv[:, 0:NB].rearrange("p (a o) -> p a o", o=1)
        r_ap = bmv[:, 256:256 + NB]
        m12_ap = bmv[:, 512:512 + NB]
        vop((lambda src=src, NB=NB: lambda e: e.tensor_reduce(
            bmv[:, 0:NB], src.rearrange("p (a b) -> p a b", b=16), axis=AX,
            op=OP.max, apply_absolute_value=True))(),
            reads=[b_src], writes=[b_bmv])
        vop((lambda NB=NB: lambda e: e.tensor_scalar(
            bmv[:, 0:NB], bmv[:, 0:NB], 6e-12, 0.0, op0=OP.max, op1=OP.add))(),
            force=[b_bmv], writes=[b_bmv])
        vop((lambda NB=NB: lambda e: e.tensor_scalar(
            bmv[:, 512:512 + NB], bmv[:, 0:NB], 1.0 / 12.0, 0.0,
            op0=OP.mult, op1=OP.add))(),
            force=[b_bmv], writes=[b_bmv])
        vop((lambda NB=NB: lambda e: e.reciprocal(bmv[:, 256:256 + NB],
                                                  bmv[:, 0:NB]))(),
            force=[b_bmv], writes=[b_bmv])
        vop((lambda src=src, r_ap=r_ap, NB=NB: lambda e: e.tensor_tensor(
            src.rearrange("p (a b) -> p a b", b=16),
            src.rearrange("p (a b) -> p a b", b=16),
            r_ap.rearrange("p (a o) -> p a o", o=1).broadcast_to([128, NB, 16]),
            op=OP.mult))(),
            force=[b_bmv], writes=[b_src])
        vop((lambda src=src, tW=tW: lambda e: e.tensor_scalar(
            tW, src, 0.0, 2.0, op0=OP.is_ge, op1=OP.mult))(),
            reads=[b_src], writes=[b_tW])
        vop((lambda tW=tW, m12_ap=m12_ap, NB=NB: lambda e: e.scalar_tensor_tensor(
            tW.rearrange("p (a b) -> p a b", b=16), tW.rearrange("p (a b) -> p a b", b=16),
            1.0, m12_ap.rearrange("p (a o) -> p a o", o=1).broadcast_to([128, NB, 16]),
            op0=OP.subtract, op1=OP.mult))(),
            force=[b_bmv], writes=[b_tW])
        sop((lambda src=src: lambda e: e.activation(src, src, AF.Abs))(),
            force=[b_src], writes=[b_src])
        vop((lambda src=src, tA=tA: lambda e: e.tensor_scalar(
            tA, src, M6[0], 1.0, op0=OP.is_gt, op1=OP.mult))(),
            reads=[b_src], writes=[b_tA])
        for j in (1, 2, 3):
            vop((lambda src=src, tA=tA, j=j: lambda e: e.scalar_tensor_tensor(
                tA, src, M6[j], tA, op0=OP.is_gt, op1=OP.add))(),
                reads=[b_src], force=[b_tA], writes=[b_tA])
        vop((lambda src=src, tB=tB: lambda e: e.tensor_scalar(
            tB, src, M6[4], 1.0, op0=OP.is_gt, op1=OP.mult))(),
            reads=[b_src], writes=[b_tB])
        vop((lambda src=src, tB=tB: lambda e: e.scalar_tensor_tensor(
            tB, src, M6[5], tB, op0=OP.is_gt, op1=OP.add))(),
            reads=[b_src], force=[b_tB], writes=[b_tB])
        vop((lambda tA=tA, tB=tB: lambda e: e.scalar_tensor_tensor(
            tA, tB, 2.0, tA, op0=OP.mult, op1=OP.add))(),
            force=[b_tA, b_tB], writes=[b_tA])
        vop((lambda src=src, tB=tB: lambda e: e.tensor_scalar(
            tB, src, M6[6], 4.0, op0=OP.is_gt, op1=OP.mult))(),
            reads=[b_src], writes=[b_tB])
        vop((lambda tA=tA, tB=tB: lambda e: e.tensor_tensor(
            tA, tA, tB, op=OP.add))(),
            force=[b_tA, b_tB], writes=[b_tA])
        vop((lambda dst=dst, tA=tA, tW=tW: lambda e: e.tensor_tensor(
            dst, tA, tW, op=OP.mult))(),
            force=[b_tA, b_tW], writes=[b_dst])

    b_cache = {"wg": _Buf("wgT_q"), "wu": _Buf("wuT_q"), "wd": _Buf("wdT_q")}
    if n_sub >= 2:
        pool_reset()
        wp_in, b_wpin = carve("wp_in", FF)
        wp_q, b_wpq = carve("wp_q", FF)
        wp_A, b_wpA = carve("wp_A", 1408)
        wp_B, b_wpB = carve("wp_B", 1408)
        wp_W, b_wpW = carve("wp_W", 1408)
        wp_bm, b_wpbm = carve("wp_bm", 768)
        wp_ev, b_wpev = carve("wp_ev", 128)
        for (wsrc, wdst, nrows, K, ckey) in ((d_wg, d_wgT, FF // 128, H, "wg"),
                                             (d_wu, d_wuT, FF // 128, H, "wu"),
                                             (d_wd, d_wdTq, H // 128, FF, "wd")):
            for rch in range(nrows):
                dma("sync", wp_in[:, 0:K], wsrc[128 * rch:128 * (rch + 1), :],
                    writes=[b_wpin])
                for sl0 in range(0, K, 1408):
                    sl = min(1408, K - sl0)
                    emit_quant(wp_in[:, sl0:sl0 + sl], b_wpin,
                               wp_q[:, sl0:sl0 + sl], b_wpq,
                               wp_A[:, 0:sl], b_wpA, wp_B[:, 0:sl], b_wpB,
                               wp_W[:, 0:sl], b_wpW, wp_bm, b_wpbm, sl)
                for c in range(K // 128):
                    top((lambda c=c, wp_q=wp_q: lambda e: e.transpose(
                        psum[0][:, 0:128], wp_q[:, 128 * c:128 * (c + 1)],
                        ident))(),
                        reads=[b_wpq, b_ident], writes=[b_ps[0]])
                    vop((lambda: lambda e: e.tensor_copy(
                        wp_ev, psum[0][:, 0:128]))(),
                        reads=[b_ps[0]], writes=[b_wpev])
                    dma("gpsimd",
                        wdst[128 * c:128 * (c + 1), 128 * rch:128 * (rch + 1)],
                        wp_ev, reads=[b_wpev], writes=[b_cache[ckey]])

    def emit_ffn(tag):
        pool_reset()
        f_xln, fb_xln = carve(tag + "xln", H)
        f_xq, fb_xq = carve(tag + "xq", 2 * H)
        f_xqT, fb_xqT = carve(tag + "xqT", 2 * H)
        f_A, fb_A = carve(tag + "A", H)
        f_B, fb_B = carve(tag + "B", H)
        f_W, fb_W = carve(tag + "W", H)
        f_bm, fb_bm = carve(tag + "bm", 768)
        f_sg0, fb_sg0 = carve(tag + "sg0", 512)
        f_sg1, fb_sg1 = carve(tag + "sg1", 512)
        f_h, fb_h = carve(tag + "h", 512)
        f_hq, fb_hq = carve(tag + "hq", 512)
        f_hqT, fb_hqT = carve(tag + "hqT", 512)
        f_wa, fb_wa = carve(tag + "wa", 512)
        f_wb, fb_wb = carve(tag + "wb", 512)
        fwbuf, fbwbuf = [f_wa, f_wb], [fb_wa, fb_wb]
        IOB = [(0, 512), (512, 512), (1024, 512), (1536, 512), (2048, 512),
               (2560, 256)]
        ln_stats_all(epso, b_epso, sqs, b_sqs)
        for g in range(NT // 2):
            for gi in range(2):
                i = 2 * g + gi
                ln_apply(i, tile_x(i), [b_xt[i]], f_xln, fb_xln)
                emit_quant(f_xln, fb_xln, f_xq[:, H * gi:H * (gi + 1)], fb_xq,
                           f_A, fb_A, f_B, fb_B, f_W, fb_W, f_bm, fb_bm, H)
                for c in range(NFC):
                    top((lambda c=c, gi=gi, f_xq=f_xq: lambda e: e.transpose(
                        psum[0][:, 0:128],
                        f_xq[:, H * gi + 128 * c:H * gi + 128 * (c + 1)],
                        ident))(),
                        reads=[fb_xq, b_ident], writes=[b_ps[0]])
                    vop((lambda c=c, gi=gi, f_xqT=f_xqT: lambda e: e.tensor_copy(
                        f_xqT[:, 128 * (2 * c + gi):128 * (2 * c + gi) + 128],
                        psum[0][:, 0:128]))(),
                        reads=[b_ps[0]], writes=[fb_xqT])
            for (io0, iow) in IOB:
                for (wcache, bwc, sgbufs) in ((d_wgT, b_cache["wg"], None),
                                               (d_wuT, b_cache["wu"], "up")):
                    for fi in range(NFC):
                        wbf, bwbf = fwbuf[fi % 2], fbwbuf[fi % 2]
                        dma("sync", wbf[:, 0:iow],
                            wcache[128 * fi:128 * (fi + 1), io0:io0 + iow],
                            reads=[bwc], writes=[bwbf])
                        for gi in range(2):
                            top((lambda fi=fi, gi=gi, wbf=wbf, iow=iow,
                                 f_xqT=f_xqT: lambda e: e.matmul(
                                     psum[2 + gi][:, 0:iow],
                                     f_xqT[:, 128 * (2 * fi + gi):128 * (2 * fi + gi) + 128],
                                     wbf[:, 0:iow],
                                     start=(fi == 0), stop=(fi == NFC - 1)))(),
                                reads=[fb_xqT, bwbf], writes=[b_ps[2 + gi]])
                    if sgbufs is None:
                        for gi, (sg, bsg) in enumerate(((f_sg0, fb_sg0),
                                                        (f_sg1, fb_sg1))):
                            sop((lambda gi=gi, sg=sg, iow=iow: lambda e: e.activation(
                                sg[:, 0:iow], psum[2 + gi][:, 0:iow], AF.Silu))(),
                                reads=[b_ps[2 + gi]], writes=[bsg])
                for gi, (sg, bsg) in enumerate(((f_sg0, fb_sg0), (f_sg1, fb_sg1))):
                    i = 2 * g + gi
                    vop((lambda gi=gi, sg=sg, iow=iow, f_h=f_h: lambda e: e.tensor_tensor(
                        f_h[:, 0:iow], sg[:, 0:iow], psum[2 + gi][:, 0:iow],
                        op=OP.mult))(),
                        reads=[bsg, b_ps[2 + gi]], writes=[fb_h])
                    emit_quant(f_h[:, 0:iow], fb_h, f_hq[:, 0:iow], fb_hq,
                               f_A[:, 0:iow], fb_A, f_B[:, 0:iow], fb_B,
                               f_W[:, 0:iow], fb_W, f_bm, fb_bm, iow)
                    for c in range(iow // 128):
                        top((lambda c=c, f_hq=f_hq: lambda e: e.transpose(
                            psum[0][:, 0:128], f_hq[:, 128 * c:128 * (c + 1)],
                            ident))(),
                            reads=[fb_hq, b_ident], writes=[b_ps[0]])
                        vop((lambda c=c, f_hqT=f_hqT: lambda e: e.tensor_copy(
                            f_hqT[:, 128 * c:128 * (c + 1)], psum[0][:, 0:128]))(),
                            reads=[b_ps[0]], writes=[fb_hqT])
                    for ii in range(iow // 128):
                        for half in range(2):
                            wbf, bwbf = fwbuf[(2 * ii + half) % 2], fbwbuf[(2 * ii + half) % 2]
                            dma("sync", wbf[:, 0:512],
                                d_wdTq[io0 + 128 * ii:io0 + 128 * (ii + 1),
                                       512 * half:512 * (half + 1)],
                                reads=[b_cache["wd"]], writes=[bwbf])
                            top((lambda ii=ii, half=half, gi=gi, wbf=wbf, io0=io0,
                                 iow=iow, f_hqT=f_hqT: lambda e: e.matmul(
                                     psum[4 + 2 * gi + half][:, :],
                                     f_hqT[:, 128 * ii:128 * (ii + 1)],
                                     wbf[:, 0:512],
                                     start=(io0 == 0 and ii == 0),
                                     stop=(io0 == 2560 and ii == iow // 128 - 1)))(),
                                reads=[fb_hqT, bwbf], writes=[b_ps[4 + 2 * gi + half]])
            for gi in range(2):
                i = 2 * g + gi
                for half in range(2):
                    vop((lambda i=i, gi=gi, half=half: lambda e: e.tensor_tensor(
                        tile_x(i)[:, 512 * half:512 * (half + 1)],
                        tile_x(i)[:, 512 * half:512 * (half + 1)],
                        psum[4 + 2 * gi + half][:, :], op=OP.add))(),
                        reads=[b_ps[4 + 2 * gi + half]], writes=[b_xt[i]])

    if n_sub >= 2:
        emit_ffn("f1")

    if n_sub >= 3:
        pool_reset()
        cbuf = []
        for ci in range(12):
            cbuf.append(carve(f"ab{ci}", H))
        r_wa, rb_wa = carve("ab_wa", 512)
        r_wb, rb_wb = carve("ab_wb", 512)
        rwbuf, rbwbuf = [r_wa, r_wb], [rb_wa, rb_wb]
        ln_stats_all(epso, b_epso, sqs, b_sqs)
        for J in range(TPB):
            (x3a, bx3a), (x3b, bx3b) = cbuf[0], cbuf[1]
            (xkf, bxkf), (xvf, bxvf) = cbuf[2], cbuf[3]
            (kfm, bkfm), (vfm, bvfm) = cbuf[4], cbuf[5]
            (ast, bast), (bst, bbst), (pst, bpst) = cbuf[6], cbuf[7], cbuf[8]
            (tp1, btp1), (tp2, btp2), (tp3, btp3) = cbuf[9], cbuf[10], cbuf[11]
            for s in range(B):
                i = TPB * s + J
                x3c, bx3c = (x3a, bx3a) if s % 2 == 0 else (x3b, bx3b)
                x3p, bx3p = (x3b, bx3b) if s % 2 == 0 else (x3a, bx3a)
                ln_apply(i, tile_x(i), [b_xt[i]], xkf, bxkf)
                for c in range(NFC):
                    top((lambda c=c, xkf=xkf: lambda e: e.transpose(
                        psum[0][:, 0:128], xkf[:, 128 * c:128 * (c + 1)], ident))(),
                        reads=[bxkf, b_ident], writes=[b_ps[0]])
                    vop((lambda c=c, x3c=x3c: lambda e: e.tensor_copy(
                        x3c[:, 128 * c:128 * (c + 1)], psum[0][:, 0:128]))(),
                        reads=[b_ps[0]], writes=[bx3c])
                if s == 0:
                    def mix0(e, x3c=x3c, xkf=xkf, xvf=xvf):
                        for c in range(NFC):
                            e.tensor_scalar(xkf[:, 128 * c:128 * (c + 1)],
                                            x3c[:, 128 * c:128 * (c + 1)],
                                            mk_s[:, c:c + 1], 0.0,
                                            op0=OP.mult, op1=OP.add)
                        last = None
                        for c in range(NFC):
                            last = e.tensor_scalar(xvf[:, 128 * c:128 * (c + 1)],
                                                   x3c[:, 128 * c:128 * (c + 1)],
                                                   mv_s[:, c:c + 1], 0.0,
                                                   op0=OP.mult, op1=OP.add)
                        return last
                    vop(mix0, force=[bx3c], reads=[b_mk, b_mv],
                        writes=[bxkf, bxvf])
                else:
                    vop((lambda x3c=x3c, x3p=x3p, tp1=tp1: lambda e: e.tensor_tensor(
                        tp1, x3c, x3p, op=OP.subtract))(),
                        force=[bx3c, bx3p], writes=[btp1])
                    def mixs(e, tp1=tp1, x3p=x3p, xkf=xkf, xvf=xvf):
                        for c in range(NFC):
                            e.scalar_tensor_tensor(xkf[:, 128 * c:128 * (c + 1)],
                                                   tp1[:, 128 * c:128 * (c + 1)],
                                                   mk_s[:, c:c + 1],
                                                   x3p[:, 128 * c:128 * (c + 1)],
                                                   op0=OP.mult, op1=OP.add)
                        last = None
                        for c in range(NFC):
                            last = e.scalar_tensor_tensor(
                                xvf[:, 128 * c:128 * (c + 1)],
                                tp1[:, 128 * c:128 * (c + 1)],
                                mv_s[:, c:c + 1],
                                x3p[:, 128 * c:128 * (c + 1)],
                                op0=OP.mult, op1=OP.add)
                        return last
                    vop(mixs, force=[btp1, bx3p], reads=[b_mk, b_mv],
                        writes=[bxkf, bxvf])
                for (wdram, xsrc, bxsrc, dstt, bdstt, bank) in (
                        (d_wkb, xkf, bxkf, kfm, bkfm, 2),
                        (d_wvb, xvf, bxvf, vfm, bvfm, 3)):
                    for ko in range(NFC):
                        for fi in range(NFC):
                            wbf, bwbf = rwbuf[fi % 2], rbwbuf[fi % 2]
                            dma("sync", wbf[:, 0:128],
                                wdram[128 * fi:128 * (fi + 1),
                                      128 * ko:128 * (ko + 1)],
                                writes=[bwbf])
                            top((lambda fi=fi, ko=ko, wbf=wbf, xsrc=xsrc,
                                 bank=bank: lambda e: e.matmul(
                                     psum[bank][:, 0:128], wbf[:, 0:128],
                                     xsrc[:, 128 * fi:128 * (fi + 1)],
                                     start=(fi == 0), stop=(fi == NFC - 1)))(),
                                reads=[bxsrc, bwbf], writes=[b_ps[bank]])
                        vop((lambda ko=ko, dstt=dstt, bank=bank: lambda e: e.tensor_copy(
                            dstt[:, 128 * ko:128 * (ko + 1)],
                            psum[bank][:, 0:128]))(),
                            reads=[b_ps[bank]], writes=[bdstt])
                if s == 0:
                    vop((lambda ast=ast, vfm=vfm: lambda e: e.tensor_copy(ast, vfm))(),
                        force=[bvfm], writes=[bast])
                    vop((lambda bst=bst: lambda e: e.memset(bst, 1.0))(),
                        writes=[bbst])
                    vop((lambda pst=pst, kfm=kfm: lambda e: e.tensor_copy(pst, kfm))(),
                        force=[bkfm], writes=[bpst])
                else:
                    def pwf(e, pst=pst, tp1=tp1):
                        last = None
                        for c in range(NFC):
                            last = e.tensor_scalar(tp1[:, 128 * c:128 * (c + 1)],
                                                   pst[:, 128 * c:128 * (c + 1)],
                                                   td_s[:, c:c + 1], 0.0,
                                                   op0=OP.add, op1=OP.add)
                        return last
                    vop(pwf, force=[bpst], reads=[b_td], writes=[btp1])
                    vop((lambda tp1=tp1, kfm=kfm, tp2=tp2: lambda e: e.tensor_tensor(
                        tp2, tp1, kfm, op=OP.max))(),
                        force=[btp1, bkfm], writes=[btp2])
                    vop((lambda tp1=tp1, tp2=tp2: lambda e: e.tensor_tensor(
                        tp1, tp1, tp2, op=OP.subtract))(),
                        force=[btp1, btp2], writes=[btp1])
                    sop((lambda tp1=tp1: lambda e: e.activation(tp1, tp1, AF.Exp))(),
                        force=[btp1], writes=[btp1])
                    vop((lambda kfm=kfm, tp2=tp2, tp3=tp3: lambda e: e.tensor_tensor(
                        tp3, kfm, tp2, op=OP.subtract))(),
                        force=[bkfm, btp2], writes=[btp3])
                    sop((lambda tp3=tp3: lambda e: e.activation(tp3, tp3, AF.Exp))(),
                        force=[btp3], writes=[btp3])
                    vop((lambda ast=ast, tp1=tp1: lambda e: e.tensor_tensor(
                        ast, ast, tp1, op=OP.mult))(),
                        force=[bast, btp1], writes=[bast])
                    vop((lambda xkf=xkf, vfm=vfm, tp3=tp3: lambda e: e.tensor_tensor(
                        xkf, vfm, tp3, op=OP.mult))(),
                        force=[bvfm, btp3], writes=[bxkf])
                    vop((lambda ast=ast, xkf=xkf: lambda e: e.tensor_tensor(
                        ast, ast, xkf, op=OP.add))(),
                        force=[bast, bxkf], writes=[bast])
                    vop((lambda bst=bst, tp1=tp1: lambda e: e.tensor_tensor(
                        bst, bst, tp1, op=OP.mult))(),
                        force=[bbst, btp1], writes=[bbst])
                    vop((lambda bst=bst, tp3=tp3: lambda e: e.tensor_tensor(
                        bst, bst, tp3, op=OP.add))(),
                        force=[bbst, btp3], writes=[bbst])
                    vop((lambda pst=pst, tp2=tp2: lambda e: e.tensor_copy(pst, tp2))(),
                        force=[btp2], writes=[bpst])
                def kkf(e, kfm=kfm, tp1=tp1):
                    last = None
                    for c in range(NFC):
                        last = e.tensor_scalar(tp1[:, 128 * c:128 * (c + 1)],
                                               kfm[:, 128 * c:128 * (c + 1)],
                                               tdtf_s[:, c:c + 1], 0.0,
                                               op0=OP.add, op1=OP.add)
                    return last
                vop(kkf, force=[bkfm, bpst], reads=[b_tdtf], writes=[btp1])
                vop((lambda pst=pst, tp1=tp1, tp2=tp2: lambda e: e.tensor_tensor(
                    tp2, pst, tp1, op=OP.max))(),
                    force=[bpst, btp1], writes=[btp2])
                vop((lambda pst=pst, tp2=tp2, tp3=tp3: lambda e: e.tensor_tensor(
                    tp3, pst, tp2, op=OP.subtract))(),
                    force=[bpst, btp2], writes=[btp3])
                sop((lambda tp3=tp3: lambda e: e.activation(tp3, tp3, AF.Exp))(),
                    force=[btp3], writes=[btp3])
                vop((lambda tp1=tp1, tp2=tp2: lambda e: e.tensor_tensor(
                    tp1, tp1, tp2, op=OP.subtract))(),
                    force=[btp1, btp2], writes=[btp1])
                sop((lambda tp1=tp1: lambda e: e.activation(tp1, tp1, AF.Exp))(),
                    force=[btp1], writes=[btp1])
                vop((lambda ast=ast, tp3=tp3, xkf=xkf: lambda e: e.tensor_tensor(
                    xkf, ast, tp3, op=OP.mult))(),
                    force=[bast, btp3], writes=[bxkf])
                vop((lambda vfm=vfm, tp1=tp1, xvf=xvf: lambda e: e.tensor_tensor(
                    xvf, vfm, tp1, op=OP.mult))(),
                    force=[bvfm, btp1], writes=[bxvf])
                vop((lambda xkf=xkf, xvf=xvf: lambda e: e.tensor_tensor(
                    xkf, xkf, xvf, op=OP.add))(),
                    force=[bxkf, bxvf], writes=[bxkf])
                vop((lambda bst=bst, tp3=tp3, tp2=tp2: lambda e: e.tensor_tensor(
                    tp2, bst, tp3, op=OP.mult))(),
                    force=[bbst, btp3], writes=[btp2])
                vop((lambda tp2=tp2, tp1=tp1: lambda e: e.tensor_tensor(
                    tp2, tp2, tp1, op=OP.add))(),
                    force=[btp2, btp1], writes=[btp2])
                vop((lambda tp2=tp2: lambda e: e.reciprocal(tp2, tp2))(),
                    force=[btp2], writes=[btp2])
                vop((lambda xkf=xkf, tp2=tp2: lambda e: e.tensor_tensor(
                    xkf, xkf, tp2, op=OP.mult))(),
                    force=[bxkf, btp2], writes=[bxkf])
                for half in range(2):
                    for fi in range(NFC):
                        wbf, bwbf = rwbuf[fi % 2], rbwbuf[fi % 2]
                        dma("sync", wbf[:, 0:512],
                            d_wob[128 * fi:128 * (fi + 1),
                                  512 * half:512 * (half + 1)],
                            writes=[bwbf])
                        top((lambda fi=fi, half=half, wbf=wbf, xkf=xkf:
                             lambda e: e.matmul(
                                 psum[4 + half][:, :],
                                 xkf[:, 128 * fi:128 * (fi + 1)],
                                 wbf[:, 0:512],
                                 start=(fi == 0), stop=(fi == NFC - 1)))(),
                            reads=[bxkf, bwbf], writes=[b_ps[4 + half]])
                for half in range(2):
                    vop((lambda i=i, half=half: lambda e: e.tensor_tensor(
                        tile_x(i)[:, 512 * half:512 * (half + 1)],
                        tile_x(i)[:, 512 * half:512 * (half + 1)],
                        psum[4 + half][:, :], op=OP.add))(),
                        reads=[b_ps[4 + half]], writes=[b_xt[i]])

    if n_sub >= 4:
        emit_ffn("f2")

    sch.phase_barrier()
    d_y4 = d_y.rearrange("(n p) h -> n p h", p=128)
    stores = []
    for i in range(NT):
        stores.append((lambda i=i: lambda e: e.dma_start(
            out=d_y4[i, :, :], in_=tile_x(i)))())
    t_st = sch.add("sync", stores, reads=b_xt, is_dma=True)
    sch.add("gpsimd", lambda e: e.engine_nop(), after=[t_st])

    sch.build(nc, stack)
    return nc


_NC_CACHE = {}


def _get_nc(n_sub):
    if n_sub not in _NC_CACHE:
        _NC_CACHE[n_sub] = build_nc(n_sub)
    return _NC_CACHE[n_sub]


def kernel(**inputs):
    x = np.ascontiguousarray(np.asarray(inputs["L0_attn_a_input_tensor"], dtype=np.float32))
    Wq = np.asarray(inputs["L0_attn_a_Wq"], dtype=np.float32)
    Wk = np.asarray(inputs["L0_attn_a_Wk"], dtype=np.float32)
    Wv = np.asarray(inputs["L0_attn_a_Wv"], dtype=np.float32)
    Wd = np.asarray(inputs["L0_attn_a_Wd"], dtype=np.float32)
    bq = np.asarray(inputs["L0_attn_a_bq"], dtype=np.float32)
    bk = np.asarray(inputs["L0_attn_a_bk"], dtype=np.float32)
    bv = np.asarray(inputs["L0_attn_a_bv"], dtype=np.float32)
    bd = np.asarray(inputs["L0_attn_a_bd"], dtype=np.float32)
    ga = np.asarray(inputs["L0_attn_a_gamma"], dtype=np.float32)
    be = np.asarray(inputs["L0_attn_a_beta"], dtype=np.float32)
    tmrkv = np.asarray(inputs["L0_attn_b_tmrkv"], dtype=np.float32)
    wkb = np.asarray(inputs["L0_attn_b_Wk"], dtype=np.float32)
    wvb = np.asarray(inputs["L0_attn_b_Wv"], dtype=np.float32)
    wob = np.asarray(inputs["L0_attn_b_Wout"], dtype=np.float32)
    td = np.asarray(inputs["L0_attn_b_time_decay"], dtype=np.float32)
    tf = np.asarray(inputs["L0_attn_b_time_first"], dtype=np.float32)
    wg = np.asarray(inputs["L0_ffn_w_gate"], dtype=np.float32)
    wu = np.asarray(inputs["L0_ffn_w_up"], dtype=np.float32)
    wd = np.asarray(inputs["L0_ffn_w_down"], dtype=np.float32)

    def sliver(v):
        return np.ascontiguousarray(v.reshape(NFC, 128).T)

    def bcast(v):
        return np.ascontiguousarray(np.broadcast_to(v, (128, H)))

    common = {
        "ident": np.eye(128, dtype=np.float32),
        "wqT": np.ascontiguousarray(Wq.T), "wkT": np.ascontiguousarray(Wk.T),
        "wvT": np.ascontiguousarray(Wv.T), "wdT": np.ascontiguousarray(Wd.T),
        "bq": sliver(bq), "bk_bc": bcast(bk), "bv_bc": bcast(bv),
        "bd_bc": bcast(bd), "ga_bc": bcast(ga), "be_bc": bcast(be),
        "wkb": np.ascontiguousarray(wkb), "wvb": np.ascontiguousarray(wvb),
        "wob": np.ascontiguousarray(wob),
        "mk": sliver(tmrkv[H:2 * H]), "mv": sliver(tmrkv[2 * H:]),
        "td": sliver(td), "tf": sliver(tf),
        "wg": np.ascontiguousarray(wg), "wu": np.ascontiguousarray(wu),
        "wd": np.ascontiguousarray(wd),
    }
    in_maps = []
    for c in range(NCORES):
        m = dict(common)
        m["x"] = np.ascontiguousarray(x[:, c * SC:(c + 1) * SC, :].reshape(T, H))
        in_maps.append(m)

    nc = _get_nc(_N_SUB)
    res = run_bass_kernel_spmd(nc, in_maps, core_ids=list(range(NCORES)))
    out = np.empty((B, S, H), dtype=np.float32)
    for c in range(NCORES):
        out[:, c * SC:(c + 1) * SC, :] = res.results[c]["y"].reshape(B, SC, H)
    return out


# version 04
# speedup vs baseline: 1.4228x; 1.4228x over previous
import contextlib
import numpy as np
import concourse.bass as bass
import concourse.mybir as mybir
from concourse.bass_utils import run_bass_kernel_spmd

F32 = mybir.dt.float32
BF16 = mybir.dt.bfloat16
OP = mybir.AluOpType
AX = mybir.AxisListType.X
AF = mybir.ActivationFunctionType

NCORES = 8
B, S, H = 4, 8192, 1024
SC = S // NCORES
T = B * SC
NT = T // 128
TPB = SC // 128
NH, HD = 16, 64
FF = 2816
NFC = H // 128
EPS_OUT = 1e-5
EPS_IN = 1e-12
EPS_ATTN = 1e-6

M6 = [m / 6.0 for m in (0.25, 0.75, 1.25, 1.75, 2.5, 3.5, 5.0)]

ENGINES = ("sync", "gpsimd", "tensor", "vector", "scalar")

_N_SUB = 4
_DEBUG_TAPS = False
_STOP_AFTER = None


class _Buf:
    __slots__ = ("last_write", "readers", "name")

    def __init__(self, name=""):
        self.last_write = None
        self.readers = []
        self.name = name


class _Task:
    __slots__ = ("engine", "fns", "deps", "fdeps", "is_dma", "is_coll",
                 "needs_inc", "val", "idx", "key", "gate")

    def __init__(self, engine, fns, deps, fdeps, is_dma, is_coll, idx):
        self.engine = engine
        self.fns = fns
        self.deps = deps
        self.fdeps = fdeps
        self.is_dma = is_dma
        self.is_coll = is_coll
        self.needs_inc = False
        self.val = 0
        self.idx = idx


class Sched:

    def __init__(self):
        self.tasks = []
        self.barrier = {}

    def add(self, engine, fns, reads=(), writes=(), after=(), force=(),
            is_dma=False, is_coll=False):
        if callable(fns):
            fns = [fns]
        deps = set()
        fdeps = set()
        for b in force:
            if b.last_write is not None:
                fdeps.add(b.last_write)
        for b in list(reads) + list(force):
            if b.last_write is not None:
                deps.add(b.last_write)
        for b in writes:
            if b.last_write is not None:
                deps.add(b.last_write)
            deps.update(b.readers)
        for t in after:
            if t is not None:
                deps.add(t)
        for t in self.barrier.values():
            deps.add(t)
        t = _Task(engine, fns, deps, fdeps, is_dma, is_coll, len(self.tasks))
        self.tasks.append(t)
        for b in list(reads) + list(force):
            b.readers.append(t)
        for b in writes:
            b.last_write = t
            b.readers = []
        return t

    def phase_barrier(self):
        last = {}
        for t in self.tasks:
            last[t.engine] = t
        self.barrier = last

    NLANES = 8

    def build(self, nc, stack):
        for t in self.tasks:
            if t.is_dma or t.is_coll:
                t.needs_inc = True
            for d in t.deps:
                assert d.idx < t.idx
                if not (d.engine == t.engine == "tensor"
                        and not (d.is_dma or d.is_coll)):
                    d.needs_inc = True
            for d in t.fdeps:
                d.needs_inc = True
        lane_ctr = {e: 0 for e in ENGINES}
        lane_cum = {e: [0] * self.NLANES for e in ENGINES}
        counts = {e: 0 for e in ENGINES}
        n_coll = 0
        for t in self.tasks:
            if t.is_dma:
                lane = lane_ctr[t.engine] % self.NLANES
                lane_ctr[t.engine] += 1
                t.key = (t.engine, lane)
                t.gate = lane_cum[t.engine][lane]
                lane_cum[t.engine][lane] += 16 * len(t.fns)
                t.val = lane_cum[t.engine][lane]
            elif t.is_coll:
                t.key = ("coll", 0)
                n_coll += 1
                t.val = n_coll
            else:
                t.key = (t.engine, -1)
                if t.needs_inc:
                    counts[t.engine] += 1
                t.val = counts[t.engine]
        sems = {}
        for e in ENGINES:
            sems[(e, -1)] = stack.enter_context(nc.semaphore(f"s_{e}"))
            for l in range(self.NLANES):
                sems[(e, l)] = stack.enter_context(nc.semaphore(f"s_{e}_l{l}"))
        sems[("coll", 0)] = stack.enter_context(nc.semaphore("s_coll"))
        per = {e: [t for t in self.tasks if t.engine == e] for e in ENGINES}
        hwm = {e: {} for e in ENGINES}

        def runner(ename):
            def body(eng):
                hw = hwm[ename]
                for t in per[ename]:
                    need = {}
                    for d in t.deps:
                        if (d.engine == ename == "tensor"
                                and not (d.is_dma or d.is_coll)
                                and d not in t.fdeps):
                            continue
                        k = d.key
                        need[k] = max(need.get(k, 0), d.val)
                    if t.is_dma and t.gate > 0:
                        k = t.key
                        need[k] = max(need.get(k, 0), t.gate)
                    for k in sorted(need):
                        v = need[k]
                        if v > hw.get(k, 0):
                            eng.wait_ge(sems[k], v)
                            hw[k] = v
                    insts = [fn(eng) for fn in t.fns]
                    if t.needs_inc:
                        if t.is_dma:
                            for i in insts:
                                i.then_inc(sems[t.key], 16)
                        elif t.is_coll:
                            insts[-1].then_inc(sems[t.key], 1)
                        else:
                            insts[-1].then_inc(sems[t.key], 1)
            return body

        with nc.Block() as block:
            block.sync(runner("sync"))
            block.gpsimd(runner("gpsimd"))
            block.tensor(runner("tensor"))
            block.vector(runner("vector"))
            block.scalar(runner("scalar"))


def build_nc(n_sub=4):
    nc = bass.Bass("TRN2", target_bir_lowering=False)
    sch = Sched()

    d_x = nc.dram_tensor("x", [T, H], F32, kind="ExternalInput").ap()
    d_y = nc.dram_tensor("y", [T, H], F32, kind="ExternalOutput").ap()
    d_ident = nc.dram_tensor("ident", [128, 128], F32, kind="ExternalInput").ap()
    d_wqT = nc.dram_tensor("wqT", [H, H], F32, kind="ExternalInput").ap()
    d_wkT = nc.dram_tensor("wkT", [H, H], F32, kind="ExternalInput").ap()
    d_wvT = nc.dram_tensor("wvT", [H, H], F32, kind="ExternalInput").ap()
    d_wdT = nc.dram_tensor("wdT", [H, H], F32, kind="ExternalInput").ap()
    d_bq = nc.dram_tensor("bq", [128, NFC], F32, kind="ExternalInput").ap()
    d_bk_bc = nc.dram_tensor("bk_bc", [128, H], F32, kind="ExternalInput").ap()
    d_bv_bc = nc.dram_tensor("bv_bc", [128, H], F32, kind="ExternalInput").ap()
    d_bd_bc = nc.dram_tensor("bd_bc", [128, H], F32, kind="ExternalInput").ap()
    d_ga_bc = nc.dram_tensor("ga_bc", [128, H], F32, kind="ExternalInput").ap()
    d_be_bc = nc.dram_tensor("be_bc", [128, H], F32, kind="ExternalInput").ap()
    d_wkb = nc.dram_tensor("wkb", [H, H], F32, kind="ExternalInput").ap()
    d_wvb = nc.dram_tensor("wvb", [H, H], F32, kind="ExternalInput").ap()
    d_wob = nc.dram_tensor("wob", [H, H], F32, kind="ExternalInput").ap()
    d_mk = nc.dram_tensor("mk", [128, NFC], F32, kind="ExternalInput").ap()
    d_mv = nc.dram_tensor("mv", [128, NFC], F32, kind="ExternalInput").ap()
    d_td = nc.dram_tensor("td", [128, NFC], F32, kind="ExternalInput").ap()
    d_tf = nc.dram_tensor("tf", [128, NFC], F32, kind="ExternalInput").ap()
    d_wg = nc.dram_tensor("wg", [FF, H], F32, kind="ExternalInput").ap()
    d_wu = nc.dram_tensor("wu", [FF, H], F32, kind="ExternalInput").ap()
    d_wd = nc.dram_tensor("wd", [H, FF], F32, kind="ExternalInput").ap()
    d_wgT = nc.dram_tensor("wgT_q", [H, FF], F32).ap()
    d_wuT = nc.dram_tensor("wuT_q", [H, FF], F32).ap()
    d_wdTq = nc.dram_tensor("wdT_q", [FF, H], F32).ap()
    d_kv_in = nc.dram_tensor("kv_in", [64, B * NH * 65], F32).ap()
    d_kv_out = nc.dram_tensor("kv_out", [64, B * NH * 65], F32).ap()
    if _DEBUG_TAPS:
        d_tap_kvi = nc.dram_tensor("tap_kvi", [64, B * NH * 65], F32, kind="ExternalOutput").ap()
        d_tap_kvo = nc.dram_tensor("tap_kvo", [64, B * NH * 65], F32, kind="ExternalOutput").ap()
        d_tap_k = nc.dram_tensor("tap_k", [128, H], F32, kind="ExternalOutput").ap()
        d_tap_v = nc.dram_tensor("tap_v", [128, NH * 65], F32, kind="ExternalOutput").ap()
        d_tap_q = nc.dram_tensor("tap_q", [128, H], F32, kind="ExternalOutput").ap()
        d_tap_ctx = nc.dram_tensor("tap_ctx", [128, H], F32, kind="ExternalOutput").ap()
        d_tap_z = nc.dram_tensor("tap_z", [128, NH], F32, kind="ExternalOutput").ap()
        d_tap_hs = nc.dram_tensor("tap_hs", [128, H], F32, kind="ExternalOutput").ap()
        d_tap_kv2 = nc.dram_tensor("tap_kv2", [128, B * 8 * 130], F32, kind="ExternalOutput").ap()
        d_tap_xln = nc.dram_tensor("tap_xln", [128, H], F32, kind="ExternalOutput").ap()
        d_tap_xlnT = nc.dram_tensor("tap_xlnT", [128, H], F32, kind="ExternalOutput").ap()
        d_tap_st = nc.dram_tensor("tap_st", [128, 2 * NT], F32, kind="ExternalOutput").ap()
        d_tap_ps = nc.dram_tensor("tap_ps", [128, 512], F32, kind="ExternalOutput").ap()
        d_tap_wb = nc.dram_tensor("tap_wb", [128, 512], F32, kind="ExternalOutput").ap()

    def sb(name, p, f, dt=F32):
        t = nc.alloc_sbuf_tensor(name, [p, f], dt)
        return t.ap(), _Buf(name)

    xres, _ = sb("xres", 128, NT * H)
    b_xt = [_Buf(f"x{i}") for i in range(NT)]
    ident, b_ident = sb("ident_t", 128, 128)
    bq_s, b_bq = sb("bq_s", 128, NFC)
    mk_s, b_mk = sb("mk_s", 128, NFC)
    mv_s, b_mv = sb("mv_s", 128, NFC)
    td_s, b_td = sb("td_s", 128, NFC)
    tf_s, b_tf = sb("tf_s", 128, NFC)
    tdtf_s, b_tdtf = sb("tdtf_s", 128, NFC)
    lnmu, b_lnmu = sb("lnmu", 128, NT)
    lnrs, b_lnrs = sb("lnrs", 128, NT)
    lnsum, b_lnsum = sb("lnsum", 128, NT)
    lnssq, b_lnssq = sb("lnssq", 128, NT)
    lnsc1, b_lnsc1 = sb("lnsc1", 128, NT)
    lnsc2, b_lnsc2 = sb("lnsc2", 128, NT)
    epso, b_epso = sb("epso", 128, 1)
    epsi, b_epsi = sb("epsi", 128, 1)
    stat, b_stat = sb("stat", 128, 8)

    POOLF = 18560
    pool_t = nc.alloc_sbuf_tensor("pool", [128, POOLF], F32)
    pool = pool_t.ap()
    _pool_off = [0]

    def carve(name, f):
        o = _pool_off[0]
        assert o + f <= POOLF, f"pool overflow at {name}: {o}+{f}"
        _pool_off[0] = o + f
        return pool[:, o:o + f], _Buf(name)

    def pool_reset():
        _pool_off[0] = 0
        sch.phase_barrier()

    stack = contextlib.ExitStack()
    psum = [stack.enter_context(nc.psum_tensor(f"ps{i}", [128, 512], F32))
            for i in range(8)]
    b_ps = [_Buf(f"ps{i}") for i in range(8)]

    def dma(engine, out_ap, in_ap, reads=(), writes=(), after=()):
        return sch.add(engine, lambda e: e.dma_start(out=out_ap, in_=in_ap),
                       reads=reads, writes=writes, after=after, is_dma=True)

    def vop(fn, reads=(), writes=(), after=(), force=(), eng="vector"):
        return sch.add(eng, fn, reads=reads, writes=writes, after=after,
                       force=force)

    def sop(fn, reads=(), writes=(), after=(), force=()):
        return sch.add("scalar", fn, reads=reads, writes=writes, after=after,
                       force=force)

    def top(fn, reads=(), writes=(), after=(), force=()):
        return sch.add("tensor", fn, reads=reads, writes=writes, after=after,
                       force=force)

    def tile_x(i):
        return xres[:, H * i: H * (i + 1)]

    dma("sync", ident, d_ident[:, :], writes=[b_ident])
    dma("sync", bq_s, d_bq[:, :], writes=[b_bq])
    dma("sync", mk_s, d_mk[:, :], writes=[b_mk])
    dma("sync", mv_s, d_mv[:, :], writes=[b_mv])
    dma("sync", td_s, d_td[:, :], writes=[b_td])
    dma("sync", tf_s, d_tf[:, :], writes=[b_tf])
    vop(lambda e: e.tensor_tensor(tdtf_s, td_s, tf_s, op=OP.add),
        reads=[b_td, b_tf], writes=[b_tdtf])
    vop(lambda e: e.memset(epso, EPS_OUT), writes=[b_epso])
    vop(lambda e: e.memset(epsi, EPS_IN), writes=[b_epsi])

    d_x4 = d_x.rearrange("(n p) h -> n p h", p=128)
    loads = []
    for i in range(NT):
        loads.append((lambda i=i: lambda e: e.dma_start(
            out=tile_x(i), in_=d_x4[i, :, :]))())
    sch.add("sync", loads, writes=b_xt, is_dma=True)

    def ln_stats_all(eps_ap, b_eps, sq_scratch, b_sq):
        for i in range(NT):
            vop((lambda i=i: lambda e: e.reduce_sum(
                lnsum[:, i:i + 1], tile_x(i), axis=AX))(),
                reads=[b_xt[i]], writes=[b_lnsum])
            sop((lambda i=i, sq=sq_scratch: lambda e: e.activation(
                sq, tile_x(i), AF.Square, accum_out=lnssq[:, i:i + 1]))(),
                reads=[b_xt[i]], force=[b_sq], writes=[b_sq, b_lnssq])
        vop(lambda e: e.tensor_scalar(lnmu, lnsum, 1.0 / H, 0.0,
                                      op0=OP.mult, op1=OP.add),
            force=[b_lnsum], writes=[b_lnmu])
        vop(lambda e: e.tensor_tensor(lnsc1, lnmu, lnmu, op=OP.mult),
            force=[b_lnmu], writes=[b_lnsc1])
        vop(lambda e: e.scalar_tensor_tensor(lnsc2, lnssq, 1.0 / H, lnsc1,
                                             op0=OP.mult, op1=OP.subtract),
            force=[b_lnssq, b_lnsc1], writes=[b_lnsc2])
        sop(lambda e: e.activation(lnsc1, lnsc2, AF.Sqrt, bias=eps_ap),
            force=[b_lnsc2], reads=[b_eps], writes=[b_lnsc1])
        vop(lambda e: e.reciprocal(lnrs, lnsc1), force=[b_lnsc1],
            writes=[b_lnrs])

    def ln_apply(i, src, b_src, dst, b_dst):
        vop((lambda i=i, src=src, dst=dst: lambda e: e.tensor_scalar(
            dst, src, lnmu[:, i:i + 1], lnrs[:, i:i + 1],
            op0=OP.subtract, op1=OP.mult))(),
            reads=list(b_src), force=[b_lnmu, b_lnrs], writes=[b_dst])

    def transpose8(src, b_src, dst, b_dst, nch=NFC):
        for c in range(nch):
            top((lambda c=c: lambda e: e.transpose(
                psum[0][:, 0:128], src[:, 128 * c:128 * (c + 1)], ident))(),
                reads=[b_src, b_ident], writes=[b_ps[0]])
            vop((lambda c=c: lambda e: e.tensor_copy(
                dst[:, 128 * c:128 * c + 128], psum[0][:, 0:128]))(),
                reads=[b_ps[0]], writes=[b_dst])

    bk_bc, b_bk = carve("bk_t", H)
    bv_bc, b_bv = carve("bv_t", H)
    dma("sync", bk_bc, d_bk_bc[:, :], writes=[b_bk])
    dma("sync", bv_bc, d_bv_bc[:, :], writes=[b_bv])
    xln, b_xln = carve("xln", H)
    sqs, b_sqs = carve("sqs", H)
    xlnT, b_xlnT = carve("xlnT", H)
    xlnT2, b_xlnT2 = carve("xlnT2", H)
    kA_sb, b_kA = carve("kA", H)
    kB_sb, b_kB = carve("kB", H)
    v65A, b_vA = carve("v65A", NH * 65)
    v65B, b_vB = carve("v65B", NH * 65)
    kvacc, b_kvacc = carve("kvacc", NH * 65)
    elu_t, b_elu = carve("elu_t", 512)
    ws0, b_ws0 = carve("ws0", 2048)
    ws1, b_ws1 = carve("ws1", 2048)
    v65A_3 = v65A.rearrange("p (h c) -> p h c", c=65)
    v65B_3 = v65B.rearrange("p (h c) -> p h c", c=65)

    vop(lambda e: e.memset(v65A_3[:, :, 64:65], 1.0), writes=[b_vA])
    vop(lambda e: e.memset(v65B_3[:, :, 64:65], 1.0), writes=[b_vB])

    KVBANK = {**{h: (1, h) for h in range(7)},
              **{h: (6, h - 7) for h in range(7, 14)},
              **{h: (7, h - 14) for h in range(14, 16)}}

    def kv_evac(jt, k_sb, b_ksb, v65, b_v65):
        for h in range(NH):
            bank, slot = KVBANK[h]
            top((lambda h=h, bank=bank, slot=slot, k_sb=k_sb, v65=v65:
                 lambda e: e.matmul(
                     psum[bank][0:64, 65 * slot:65 * (slot + 1)],
                     k_sb[:, 64 * h:64 * (h + 1)], v65[:, 65 * h:65 * (h + 1)],
                     start=True, stop=True))(),
                reads=[b_ksb, b_v65], writes=[b_ps[bank]])
        for bank, off, nh_in in ((1, 0, 7), (6, 7, 7), (7, 14, 2)):
            if jt == 0:
                vop((lambda bank=bank, off=off, nh_in=nh_in: lambda e: e.tensor_copy(
                    kvacc[0:64, 65 * off:65 * (off + nh_in)],
                    psum[bank][0:64, 0:65 * nh_in]))(),
                    reads=[b_ps[bank]], writes=[b_kvacc])
            else:
                vop((lambda bank=bank, off=off, nh_in=nh_in: lambda e: e.tensor_tensor(
                    kvacc[0:64, 65 * off:65 * (off + nh_in)],
                    kvacc[0:64, 65 * off:65 * (off + nh_in)],
                    psum[bank][0:64, 0:65 * nh_in], op=OP.add))(),
                    reads=[b_ps[bank]], force=[b_kvacc], writes=[b_kvacc])

    kv_dmas = []
    ln_stats_all(epso, b_epso, sqs, b_sqs)
    for b in range(B):
        for jp in range(TPB // 2):
            tiles = (TPB * b + 2 * jp, TPB * b + 2 * jp + 1)
            kbufs = ((kA_sb, b_kA), (kB_sb, b_kB))
            vbufs = ((v65A, b_vA, v65A_3), (v65B, b_vB, v65B_3))
            xlnTs = {}
            for ti, i in enumerate(tiles):
                pass
            ln_apply(tiles[0], tile_x(tiles[0]), [b_xt[tiles[0]]], xln, b_xln)
            transpose8(xln, b_xln, xlnT, b_xlnT)
            ln_apply(tiles[1], tile_x(tiles[1]), [b_xt[tiles[1]]], xln, b_xln)
            transpose8(xln, b_xln, xlnT2, b_xlnT2)
            for (wdram, bias_bc, b_bias, is_k) in (
                    (d_wkT, bk_bc, b_bk, True), (d_wvT, bv_bc, b_bv, False)):
                for half in range(2):
                    for si, (ws, bws) in enumerate(((ws0, b_ws0), (ws1, b_ws1))):
                        dma("sync", ws.rearrange("p (c k) -> p c k", k=512),
                            wdram.rearrange("(c p) k -> c p k", p=128)
                            [4 * si:4 * (si + 1), :, 512 * half:512 * (half + 1)]
                            .rearrange("c p k -> p c k"),
                            writes=[bws])
                    for ti, i in enumerate(tiles):
                        xT = xlnT if ti == 0 else xlnT2
                        bxT = b_xlnT if ti == 0 else b_xlnT2
                        pb = b_ps[2 + 2 * ti + half] if False else b_ps[2 + ti]
                        ps_ap = psum[2 + ti]
                        for fi in range(NFC):
                            ws, bws = (ws0, b_ws0) if fi < 4 else (ws1, b_ws1)
                            top((lambda fi=fi, ws=ws, ps_ap=ps_ap, xT=xT:
                                 lambda e: e.matmul(
                                     ps_ap[:, :], xT[:, 128 * fi:128 * (fi + 1)],
                                     ws[:, 512 * (fi % 4):512 * (fi % 4 + 1)],
                                     start=(fi == 0), stop=(fi == NFC - 1)))(),
                                reads=[bxT, bws], writes=[pb])
                        k_sb, b_ksb = kbufs[ti]
                        v65, b_v65, v65_3 = vbufs[ti]
                        if is_k:
                            vop((lambda ps_ap=ps_ap, half=half, bias_bc=bias_bc,
                                 k_sb=k_sb: lambda e: e.scalar_tensor_tensor(
                                     k_sb[:, 512 * half:512 * (half + 1)],
                                     ps_ap[:, :], 0.0,
                                     bias_bc[:, 512 * half:512 * (half + 1)],
                                     op0=OP.bypass, op1=OP.add))(),
                                reads=[pb, b_bias], writes=[b_ksb])
                            vop((lambda half=half, k_sb=k_sb: lambda e: e.tensor_scalar(
                                elu_t, k_sb[:, 512 * half:512 * (half + 1)],
                                0.0, 0.0, op0=OP.min, op1=OP.add))(),
                                reads=[b_ksb], writes=[b_elu])
                            sop(lambda e: e.activation(elu_t, elu_t, AF.Exp),
                                force=[b_elu], writes=[b_elu])
                            vop((lambda half=half, k_sb=k_sb: lambda e: e.scalar_tensor_tensor(
                                k_sb[:, 512 * half:512 * (half + 1)],
                                k_sb[:, 512 * half:512 * (half + 1)], 0.0,
                                elu_t, op0=OP.max, op1=OP.add))(),
                                force=[b_elu], writes=[b_ksb])
                        else:
                            vop((lambda ps_ap=ps_ap, half=half, bias_bc=bias_bc,
                                 v65_3=v65_3: lambda e: e.scalar_tensor_tensor(
                                     v65_3[:, 8 * half:8 * (half + 1), 0:64],
                                     ps_ap.rearrange("p (h c) -> p h c", c=64), 0.0,
                                     bias_bc.rearrange("p (h c) -> p h c", c=64)
                                     [:, 8 * half:8 * (half + 1), :],
                                     op0=OP.bypass, op1=OP.add))(),
                                reads=[pb, b_bias], writes=[b_v65])
            for ti, i in enumerate(tiles):
                jt = 2 * jp + ti
                k_sb, b_ksb = kbufs[ti]
                v65, b_v65, _v3 = vbufs[ti]
                kv_evac(jt, k_sb, b_ksb, v65, b_v65)
        dma_t = dma("gpsimd", d_kv_in[:, 1040 * b:1040 * (b + 1)],
                    kvacc[0:64, :], reads=[b_kvacc])
        kv_dmas.append(dma_t)

    t_ar = sch.add("gpsimd", lambda e: e.collective_compute(
        "AllReduce", OP.add, replica_groups=[list(range(NCORES))],
        ins=[d_kv_in[:, :].opt()], outs=[d_kv_out[:, :].opt()]),
        after=kv_dmas, is_coll=True)

    if _STOP_AFTER == "ar":
        sch.add("gpsimd", lambda e: e.engine_nop(), after=[t_ar])
        sch.build(nc, stack)
        return nc
    pool_reset()
    bd_bc, b_bd = carve("bd_t", H)
    ga_bc, b_ga = carve("ga_t", H)
    be_bc, b_be = carve("be_t", H)
    dma("sync", bd_bc, d_bd_bc[:, :], writes=[b_bd])
    dma("sync", ga_bc, d_ga_bc[:, :], writes=[b_ga])
    dma("sync", be_bc, d_be_bc[:, :], writes=[b_be])
    p2_xln, p2_bxln = carve("xlnB", H)
    p2_xlnT, p2_bxlnT = carve("xlnTB", H)
    qfm, b_qfm = carve("qfm", H)
    kv2, b_kv2 = carve("kv2", B * 8 * 130)
    kvst2, b_kvst2 = carve("kvst2", NH * 65)
    ctx_sb, b_ctx = carve("ctx_sb", H)
    zden, b_zden = carve("zden", NH)
    elu2, b_elu2 = carve("elu2", H)
    hs_sb, b_hs = carve("hs_sb", H)
    qs0, b_qs0 = carve("qs0", 1024)
    qs1, b_qs1 = carve("qs1", 1024)
    wdsl, b_wdsl = carve("wdsl", 2048)
    ctxT, b_ctxT = qfm, b_qfm

    if _STOP_AFTER == "kv2m":
        sch.add("gpsimd", lambda e: e.engine_nop(), after=[t_ar])
        sch.build(nc, stack)
        return nc
    if _STOP_AFTER == "kv2z":
        vop(lambda e: e.memset(pool[:, 12500:12600], 0.0), writes=[b_kv2], after=[t_ar])
        sch.add("gpsimd", lambda e: e.engine_nop(), after=[t_ar, b_kv2.last_write])
        sch.build(nc, stack)
        return nc
    vop(lambda e: e.memset(kv2, 0.0), writes=[b_kv2], after=[t_ar])
    if _STOP_AFTER == "kv2a":
        sch.add("gpsimd", lambda e: e.engine_nop(), after=[t_ar, b_kv2.last_write])
        sch.build(nc, stack)
        return nc
    for b in range(B):
        dma("gpsimd", kvst2[0:64, :], d_kv_out[:, 1040 * b:1040 * (b + 1)],
            writes=[b_kvst2], after=[t_ar])

        def mk_build(b=b):
            def f(e):
                last = None
                for p in range(8):
                    base = 1040 * b + 130 * p
                    last = e.tensor_copy(kv2[0:64, base:base + 65],
                                         kvst2[0:64, 130 * p:130 * p + 65])
                    last = e.tensor_copy(kv2[64:128, base + 65:base + 130],
                                         kvst2[0:64, 130 * p + 65:130 * (p + 1)])
                return last
            return f
        vop(mk_build(), reads=[b_kvst2], writes=[b_kv2])

    def eps_f(e):
        v3 = kv2.rearrange("p (x c) -> p x c", c=130)
        e.tensor_scalar(v3[0:64, :, 64:65], v3[0:64, :, 64:65], EPS_ATTN, 0.0,
                        op0=OP.add, op1=OP.add)
        return e.tensor_scalar(v3[64:128, :, 129:130], v3[64:128, :, 129:130],
                               EPS_ATTN, 0.0, op0=OP.add, op1=OP.add)
    vop(eps_f, force=[b_kv2], writes=[b_kv2])
    if _STOP_AFTER == "kv2":
        sch.build(nc, stack)
        return nc

    CTXBANK = [(4 + p // 3, p % 3) for p in range(8)]

    for i in range(NT):
        b = i // TPB
        ln_apply(i, tile_x(i), [b_xt[i]], p2_xln, p2_bxln)
        transpose8(p2_xln, p2_bxln, p2_xlnT, p2_bxlnT)
        for c in range(NFC):
            pb = b_ps[2 + c % 2]
            ps_ap = psum[2 + c % 2]
            qs, bqs = (qs0, b_qs0) if c % 2 == 0 else (qs1, b_qs1)
            dma("sync", qs.rearrange("p (f k) -> p f k", k=128),
                d_wqT.rearrange("(f p) k -> f p k", p=128)
                [:, :, 128 * c:128 * (c + 1)].rearrange("f p k -> p f k"),
                writes=[bqs])
            for fi in range(NFC):
                top((lambda fi=fi, qs=qs, ps_ap=ps_ap: lambda e: e.matmul(
                    ps_ap[:, 0:128], qs[:, 128 * fi:128 * (fi + 1)],
                    p2_xlnT[:, 128 * fi:128 * (fi + 1)],
                    start=(fi == 0), stop=(fi == NFC - 1)))(),
                    reads=[p2_bxlnT, bqs], writes=[pb])
            vop((lambda c=c, ps_ap=ps_ap: lambda e: e.tensor_scalar(
                elu2[:, 128 * c:128 * (c + 1)], ps_ap[:, 0:128],
                bq_s[:, c:c + 1], 0.0, op0=OP.add, op1=OP.min))(),
                reads=[pb, b_bq], writes=[b_elu2])
            sop((lambda c=c: lambda e: e.activation(
                elu2[:, 128 * c:128 * (c + 1)], elu2[:, 128 * c:128 * (c + 1)],
                AF.Exp))(), reads=[b_elu2], writes=[b_elu2])
            sop((lambda c=c, ps_ap=ps_ap: lambda e: e.activation(
                qfm[:, 128 * c:128 * (c + 1)], ps_ap[:, 0:128], AF.Relu,
                bias=bq_s[:, c:c + 1]))(), reads=[pb, b_bq], writes=[b_qfm])
            vop((lambda c=c: lambda e: e.tensor_tensor(
                qfm[:, 128 * c:128 * (c + 1)], qfm[:, 128 * c:128 * (c + 1)],
                elu2[:, 128 * c:128 * (c + 1)], op=OP.add))(),
                reads=[b_elu2], writes=[b_qfm])
        for p in range(8):
            bank, slot = CTXBANK[p]
            top((lambda p=p, b=b, bank=bank, slot=slot: lambda e: e.matmul(
                psum[bank][:, 130 * slot:130 * (slot + 1)],
                qfm[:, 128 * p:128 * (p + 1)],
                kv2[:, 1040 * b + 130 * p:1040 * b + 130 * (p + 1)],
                start=True, stop=True))(),
                reads=[b_qfm, b_kv2], writes=[b_ps[bank]])

        def gden(e):
            last = None
            for p in range(8):
                bank, slot = CTXBANK[p]
                last = e.tensor_copy(zden[:, 2 * p:2 * p + 1],
                                     psum[bank][:, 130 * slot + 64:130 * slot + 65])
                last = e.tensor_copy(zden[:, 2 * p + 1:2 * p + 2],
                                     psum[bank][:, 130 * slot + 129:130 * slot + 130])
            return last
        vop(gden, reads=[b_ps[4], b_ps[5], b_ps[6]], writes=[b_zden])
        vop(lambda e: e.reciprocal(zden, zden), force=[b_zden], writes=[b_zden])

        def gctx(e):
            last = None
            for p in range(8):
                bank, slot = CTXBANK[p]
                last = e.tensor_scalar(ctx_sb[:, 128 * p:128 * p + 64],
                                       psum[bank][:, 130 * slot:130 * slot + 64],
                                       zden[:, 2 * p:2 * p + 1], 0.0,
                                       op0=OP.mult, op1=OP.add)
                last = e.tensor_scalar(ctx_sb[:, 128 * p + 64:128 * (p + 1)],
                                       psum[bank][:, 130 * slot + 65:130 * slot + 129],
                                       zden[:, 2 * p + 1:2 * p + 2], 0.0,
                                       op0=OP.mult, op1=OP.add)
            return last
        vop(gctx, reads=[b_ps[4], b_ps[5], b_ps[6]], force=[b_zden], writes=[b_ctx])
        if _DEBUG_TAPS and i == 0:
            dma("gpsimd", d_tap_q[:, :], qfm, reads=[b_qfm])
            dma("gpsimd", d_tap_ctx[:, :], ctx_sb, reads=[b_ctx])
            dma("gpsimd", d_tap_z[:, :], zden, reads=[b_zden])
            dma("gpsimd", d_tap_kv2[:, :], kv2, reads=[b_kv2])
        transpose8(ctx_sb, b_ctx, ctxT, b_ctxT)
        for half in range(2):
            pb = b_ps[2 + half]
            ps_ap = psum[2 + half]
            for si in range(2):
                dma("sync", wdsl.rearrange("p (f k) -> p f k", k=512),
                    d_wdT.rearrange("(f p) k -> f p k", p=128)
                    [4 * si:4 * (si + 1), :, 512 * half:512 * (half + 1)]
                    .rearrange("f p k -> p f k"),
                    writes=[b_wdsl])
                for fj in range(4):
                    fi = 4 * si + fj
                    top((lambda fi=fi, fj=fj, ps_ap=ps_ap: lambda e: e.matmul(
                        ps_ap[:, :], ctxT[:, 128 * fi:128 * (fi + 1)],
                        wdsl[:, 512 * fj:512 * (fj + 1)],
                        start=(fi == 0), stop=(fi == NFC - 1)))(),
                        reads=[b_ctxT, b_wdsl], writes=[pb])
            vop((lambda half=half, ps_ap=ps_ap: lambda e: e.scalar_tensor_tensor(
                hs_sb[:, 512 * half:512 * (half + 1)], ps_ap[:, :], 0.0,
                bd_bc[:, 512 * half:512 * (half + 1)],
                op0=OP.bypass, op1=OP.add))(),
                reads=[b_ps[2 + half], b_bd], writes=[b_hs])
        vop(lambda e: e.tensor_tensor(hs_sb, hs_sb, p2_xln, op=OP.add),
            reads=[p2_bxln], writes=[b_hs])
        if _DEBUG_TAPS and i == 0:
            dma("gpsimd", d_tap_hs[:, :], hs_sb, reads=[b_hs])
        vop(lambda e: e.reduce_sum(stat[:, 0:1], hs_sb, axis=AX),
            reads=[b_hs], writes=[b_stat])
        sop(lambda e: e.activation(elu2, hs_sb, AF.Square, accum_out=stat[:, 1:2]),
            reads=[b_hs], writes=[b_elu2, b_stat])
        vop(lambda e: e.tensor_scalar(stat[:, 5:6], stat[:, 0:1], 1.0 / H, 0.0,
                                      op0=OP.mult, op1=OP.add),
            force=[b_stat], writes=[b_stat])
        vop(lambda e: e.tensor_tensor(stat[:, 2:3], stat[:, 5:6], stat[:, 5:6],
                                      op=OP.mult),
            force=[b_stat], writes=[b_stat])
        vop(lambda e: e.scalar_tensor_tensor(stat[:, 3:4], stat[:, 1:2], 1.0 / H,
                                             stat[:, 2:3], op0=OP.mult,
                                             op1=OP.subtract),
            force=[b_stat], writes=[b_stat])
        sop(lambda e: e.activation(stat[:, 4:5], stat[:, 3:4], AF.Sqrt, bias=epsi),
            force=[b_stat], reads=[b_epsi], writes=[b_stat])
        vop(lambda e: e.reciprocal(stat[:, 6:7], stat[:, 4:5]),
            force=[b_stat], writes=[b_stat])
        vop(lambda e: e.tensor_scalar(hs_sb, hs_sb, stat[:, 5:6], stat[:, 6:7],
                                      op0=OP.subtract, op1=OP.mult),
            force=[b_stat], writes=[b_hs])
        vop(lambda e: e.tensor_tensor(hs_sb, hs_sb, ga_bc, op=OP.mult),
            reads=[b_ga], writes=[b_hs])
        vop(lambda e: e.tensor_tensor(hs_sb, hs_sb, be_bc, op=OP.add),
            reads=[b_be], writes=[b_hs])
        vop((lambda i=i: lambda e: e.tensor_tensor(
            tile_x(i), tile_x(i), hs_sb, op=OP.add))(),
            reads=[b_hs], writes=[b_xt[i]])

    def emit_quant(src, b_src, dst, b_dst, tA, b_tA, tB, b_tB, tW, b_tW,
                   bmv, b_bmv, D, NBcap):
        NB = D // 16
        r_ap = bmv[:, NBcap:NBcap + NB]
        m12_ap = bmv[:, 2 * NBcap:2 * NBcap + NB]
        vop((lambda src=src, NB=NB: lambda e: e.tensor_reduce(
            bmv[:, 0:NB], src.rearrange("p (a b) -> p a b", b=16), axis=AX,
            op=OP.max, apply_absolute_value=True))(),
            reads=[b_src], writes=[b_bmv])
        vop((lambda NB=NB: lambda e: e.tensor_scalar(
            bmv[:, 0:NB], bmv[:, 0:NB], 6e-12, 0.0, op0=OP.max, op1=OP.add))(),
            force=[b_bmv], writes=[b_bmv])
        vop((lambda NB=NB, NBcap=NBcap: lambda e: e.tensor_scalar(
            bmv[:, 2 * NBcap:2 * NBcap + NB], bmv[:, 0:NB], 1.0 / 12.0, 0.0,
            op0=OP.mult, op1=OP.add))(),
            force=[b_bmv], writes=[b_bmv])
        vop((lambda NB=NB, NBcap=NBcap: lambda e: e.reciprocal(
            bmv[:, NBcap:NBcap + NB], bmv[:, 0:NB]))(),
            force=[b_bmv], writes=[b_bmv])
        vop((lambda src=src, r_ap=r_ap, NB=NB: lambda e: e.tensor_tensor(
            src.rearrange("p (a b) -> p a b", b=16),
            src.rearrange("p (a b) -> p a b", b=16),
            r_ap.rearrange("p (a o) -> p a o", o=1).broadcast_to([128, NB, 16]),
            op=OP.mult))(),
            force=[b_bmv], writes=[b_src])
        vop((lambda src=src, tW=tW: lambda e: e.tensor_scalar(
            tW, src, 0.0, 2.0, op0=OP.is_ge, op1=OP.mult))(),
            reads=[b_src], writes=[b_tW])
        vop((lambda tW=tW, m12_ap=m12_ap, NB=NB: lambda e: e.scalar_tensor_tensor(
            tW.rearrange("p (a b) -> p a b", b=16), tW.rearrange("p (a b) -> p a b", b=16),
            1.0, m12_ap.rearrange("p (a o) -> p a o", o=1).broadcast_to([128, NB, 16]),
            op0=OP.subtract, op1=OP.mult))(),
            force=[b_bmv], writes=[b_tW])
        sop((lambda src=src: lambda e: e.activation(src, src, AF.Abs))(),
            force=[b_src], writes=[b_src])
        vop((lambda src=src, tA=tA: lambda e: e.tensor_scalar(
            tA, src, M6[0], 1.0, op0=OP.is_gt, op1=OP.mult))(),
            reads=[b_src], writes=[b_tA])
        for j in (1, 2, 3):
            vop((lambda src=src, tA=tA, j=j: lambda e: e.scalar_tensor_tensor(
                tA, src, M6[j], tA, op0=OP.is_gt, op1=OP.add))(),
                reads=[b_src], force=[b_tA], writes=[b_tA])
        vop((lambda src=src, tB=tB: lambda e: e.tensor_scalar(
            tB, src, M6[4], 1.0, op0=OP.is_gt, op1=OP.mult))(),
            reads=[b_src], writes=[b_tB])
        vop((lambda src=src, tB=tB: lambda e: e.scalar_tensor_tensor(
            tB, src, M6[5], tB, op0=OP.is_gt, op1=OP.add))(),
            reads=[b_src], force=[b_tB], writes=[b_tB])
        vop((lambda tA=tA, tB=tB: lambda e: e.scalar_tensor_tensor(
            tA, tB, 2.0, tA, op0=OP.mult, op1=OP.add))(),
            force=[b_tA, b_tB], writes=[b_tA])
        vop((lambda src=src, tB=tB: lambda e: e.tensor_scalar(
            tB, src, M6[6], 4.0, op0=OP.is_gt, op1=OP.mult))(),
            reads=[b_src], writes=[b_tB])
        vop((lambda tA=tA, tB=tB: lambda e: e.tensor_tensor(
            tA, tA, tB, op=OP.add))(),
            force=[b_tA, b_tB], writes=[b_tA])
        vop((lambda dst=dst, tA=tA, tW=tW: lambda e: e.tensor_tensor(
            dst, tA, tW, op=OP.mult))(),
            force=[b_tA, b_tW], writes=[b_dst])

    b_cache = {"wg": _Buf("wgT_q"), "wu": _Buf("wuT_q"), "wd": _Buf("wdT_q")}
    if n_sub >= 2:
        pool_reset()
        wp_in, b_wpin = carve("wp_in", FF)
        wp_q, b_wpq = carve("wp_q", FF)
        wp_A, b_wpA = carve("wp_A", 1408)
        wp_B, b_wpB = carve("wp_B", 1408)
        wp_W, b_wpW = carve("wp_W", 1408)
        wp_bm, b_wpbm = carve("wp_bm", 768)
        wp_ev, b_wpev = carve("wp_ev", 128)
        for (wsrc, wdst, nrows, K, ckey) in ((d_wg, d_wgT, FF // 128, H, "wg"),
                                             (d_wu, d_wuT, FF // 128, H, "wu"),
                                             (d_wd, d_wdTq, H // 128, FF, "wd")):
            for rch in range(nrows):
                dma("sync", wp_in[:, 0:K], wsrc[128 * rch:128 * (rch + 1), :],
                    writes=[b_wpin])
                for sl0 in range(0, K, 1408):
                    sl = min(1408, K - sl0)
                    emit_quant(wp_in[:, sl0:sl0 + sl], b_wpin,
                               wp_q[:, sl0:sl0 + sl], b_wpq,
                               wp_A[:, 0:sl], b_wpA, wp_B[:, 0:sl], b_wpB,
                               wp_W[:, 0:sl], b_wpW, wp_bm, b_wpbm, sl, 176)
                for c in range(K // 128):
                    top((lambda c=c, wp_q=wp_q: lambda e: e.transpose(
                        psum[0][:, 0:128], wp_q[:, 128 * c:128 * (c + 1)],
                        ident))(),
                        reads=[b_wpq, b_ident], writes=[b_ps[0]])
                    vop((lambda: lambda e: e.tensor_copy(
                        wp_ev, psum[0][:, 0:128]))(),
                        reads=[b_ps[0]], writes=[b_wpev])
                    dma("gpsimd",
                        wdst[128 * c:128 * (c + 1), 128 * rch:128 * (rch + 1)],
                        wp_ev, reads=[b_wpev], writes=[b_cache[ckey]])

    def emit_ffn(tag):
        pool_reset()
        f_sqs, fb_sqs = carve(tag + "sqs", H)
        f_xln, fb_xln = carve(tag + "xln", H)
        f_xq, fb_xq = carve(tag + "xq", 2 * H)
        f_xqT, fb_xqT = carve(tag + "xqT", 2 * H)
        f_A, fb_A = carve(tag + "A", H)
        f_B, fb_B = carve(tag + "B", H)
        f_W, fb_W = carve(tag + "W", H)
        f_bm, fb_bm = carve(tag + "bm", 192)
        f_sg0, fb_sg0 = carve(tag + "sg0", 512)
        f_sg1, fb_sg1 = carve(tag + "sg1", 512)
        f_h, fb_h = carve(tag + "h", 512)
        f_hq, fb_hq = carve(tag + "hq", 512)
        f_hqT, fb_hqT = carve(tag + "hqT", 512)
        slabs = {}
        for role in ("g", "u", "d"):
            slabs[role] = [carve(tag + role + "0", 1024),
                           carve(tag + role + "1", 1024)]
        IOB = [(0, 512), (512, 512), (1024, 512), (1536, 512), (2048, 512),
               (2560, 256)]
        ln_stats_all(epso, b_epso, f_sqs, fb_sqs)
        for g in range(NT // 2):
            for gi in range(2):
                i = 2 * g + gi
                ln_apply(i, tile_x(i), [b_xt[i]], f_xln, fb_xln)
                emit_quant(f_xln, fb_xln, f_xq[:, H * gi:H * (gi + 1)], fb_xq,
                           f_A, fb_A, f_B, fb_B, f_W, fb_W, f_bm, fb_bm, H, 64)
                for c in range(NFC):
                    top((lambda c=c, gi=gi, f_xq=f_xq: lambda e: e.transpose(
                        psum[0][:, 0:128],
                        f_xq[:, H * gi + 128 * c:H * gi + 128 * (c + 1)],
                        ident))(),
                        reads=[fb_xq, b_ident], writes=[b_ps[0]])
                    vop((lambda c=c, gi=gi, f_xqT=f_xqT: lambda e: e.tensor_copy(
                        f_xqT[:, 128 * (2 * c + gi):128 * (2 * c + gi) + 128],
                        psum[0][:, 0:128]))(),
                        reads=[b_ps[0]], writes=[fb_xqT])
            for (io0, iow) in IOB:
                for (wcache, bwc, role, sgbufs) in (
                        (d_wgT, b_cache["wg"], "g", None),
                        (d_wuT, b_cache["wu"], "u", "up")):
                    for fp in range(4):
                        (ws, bws) = slabs[role][fp % 2]
                        dma("sync",
                            ws.rearrange("p (f k) -> p f k", k=iow)[:, 0:2, :],
                            wcache.rearrange("(f p) k -> f p k", p=128)
                            [2 * fp:2 * (fp + 1), :, io0:io0 + iow]
                            .rearrange("f p k -> p f k"),
                            reads=[bwc], writes=[bws])
                        for fj in range(2):
                            fi = 2 * fp + fj
                            for gi in range(2):
                                top((lambda fi=fi, fj=fj, gi=gi, ws=ws, iow=iow,
                                     f_xqT=f_xqT: lambda e: e.matmul(
                                         psum[2 + gi][:, 0:iow],
                                         f_xqT[:, 128 * (2 * fi + gi):128 * (2 * fi + gi) + 128],
                                         ws[:, iow * fj:iow * (fj + 1)],
                                         start=(fi == 0), stop=(fi == NFC - 1)))(),
                                    reads=[fb_xqT, bws], writes=[b_ps[2 + gi]])
                    if sgbufs is None:
                        for gi, (sg, bsg) in enumerate(((f_sg0, fb_sg0),
                                                        (f_sg1, fb_sg1))):
                            sop((lambda gi=gi, sg=sg, iow=iow: lambda e: e.activation(
                                sg[:, 0:iow], psum[2 + gi][:, 0:iow], AF.Silu))(),
                                reads=[b_ps[2 + gi]], writes=[bsg])
                for gi, (sg, bsg) in enumerate(((f_sg0, fb_sg0), (f_sg1, fb_sg1))):
                    i = 2 * g + gi
                    vop((lambda gi=gi, sg=sg, iow=iow, f_h=f_h: lambda e: e.tensor_tensor(
                        f_h[:, 0:iow], sg[:, 0:iow], psum[2 + gi][:, 0:iow],
                        op=OP.mult))(),
                        reads=[bsg, b_ps[2 + gi]], writes=[fb_h])
                    emit_quant(f_h[:, 0:iow], fb_h, f_hq[:, 0:iow], fb_hq,
                               f_A[:, 0:iow], fb_A, f_B[:, 0:iow], fb_B,
                               f_W[:, 0:iow], fb_W, f_bm, fb_bm, iow, 64)
                    for c in range(iow // 128):
                        top((lambda c=c, f_hq=f_hq: lambda e: e.transpose(
                            psum[0][:, 0:128], f_hq[:, 128 * c:128 * (c + 1)],
                            ident))(),
                            reads=[fb_hq, b_ident], writes=[b_ps[0]])
                        vop((lambda c=c, f_hqT=f_hqT: lambda e: e.tensor_copy(
                            f_hqT[:, 128 * c:128 * (c + 1)], psum[0][:, 0:128]))(),
                            reads=[b_ps[0]], writes=[fb_hqT])
                    for ip in range((iow // 128 + 1) // 2):
                        nii = min(2, iow // 128 - 2 * ip)
                        for half in range(2):
                            (ws, bws) = slabs["d"][(2 * ip + half) % 2]
                            dma("sync",
                                ws.rearrange("p (f k) -> p f k", k=512)[:, 0:nii, :],
                                d_wdTq.rearrange("(f p) k -> f p k", p=128)
                                [(io0 // 128) + 2 * ip:(io0 // 128) + 2 * ip + nii,
                                 :, 512 * half:512 * (half + 1)]
                                .rearrange("f p k -> p f k"),
                                reads=[b_cache["wd"]], writes=[bws])
                            for ij in range(nii):
                                ii = 2 * ip + ij
                                top((lambda ii=ii, ij=ij, half=half, gi=gi, ws=ws,
                                     io0=io0, iow=iow, f_hqT=f_hqT: lambda e: e.matmul(
                                         psum[4 + 2 * gi + half][:, :],
                                         f_hqT[:, 128 * ii:128 * (ii + 1)],
                                         ws[:, 512 * ij:512 * (ij + 1)],
                                         start=(io0 == 0 and ii == 0),
                                         stop=(io0 == 2560 and ii == iow // 128 - 1)))(),
                                    reads=[fb_hqT, bws], writes=[b_ps[4 + 2 * gi + half]])
            for gi in range(2):
                i = 2 * g + gi
                for half in range(2):
                    vop((lambda i=i, gi=gi, half=half: lambda e: e.tensor_tensor(
                        tile_x(i)[:, 512 * half:512 * (half + 1)],
                        tile_x(i)[:, 512 * half:512 * (half + 1)],
                        psum[4 + 2 * gi + half][:, :], op=OP.add))(),
                        reads=[b_ps[4 + 2 * gi + half]], writes=[b_xt[i]])

    if n_sub >= 2:
        emit_ffn("f1")

    if n_sub >= 3:
        pool_reset()
        ab_sqs, ab_bsqs = carve("ab_sqs", H)
        cbuf = []
        for ci in range(12):
            cbuf.append(carve(f"ab{ci}", H))
        kv_sl = [carve("ab_kv0", 1024), carve("ab_kv1", 1024)]
        wo_sl, bwo_sl = carve("ab_wo", 2048)
        ln_stats_all(epso, b_epso, ab_sqs, ab_bsqs)
        for J in range(TPB):
            (x3a, bx3a), (x3b, bx3b) = cbuf[0], cbuf[1]
            (xkf, bxkf), (xvf, bxvf) = cbuf[2], cbuf[3]
            (kfm, bkfm), (vfm, bvfm) = cbuf[4], cbuf[5]
            (ast, bast), (bst, bbst), (pst, bpst) = cbuf[6], cbuf[7], cbuf[8]
            (tp1, btp1), (tp2, btp2), (tp3, btp3) = cbuf[9], cbuf[10], cbuf[11]
            for s in range(B):
                i = TPB * s + J
                x3c, bx3c = (x3a, bx3a) if s % 2 == 0 else (x3b, bx3b)
                x3p, bx3p = (x3b, bx3b) if s % 2 == 0 else (x3a, bx3a)
                ln_apply(i, tile_x(i), [b_xt[i]], xkf, bxkf)
                for c in range(NFC):
                    top((lambda c=c, xkf=xkf: lambda e: e.transpose(
                        psum[0][:, 0:128], xkf[:, 128 * c:128 * (c + 1)], ident))(),
                        reads=[bxkf, b_ident], writes=[b_ps[0]])
                    vop((lambda c=c, x3c=x3c: lambda e: e.tensor_copy(
                        x3c[:, 128 * c:128 * (c + 1)], psum[0][:, 0:128]))(),
                        reads=[b_ps[0]], writes=[bx3c])
                if s == 0:
                    def mix0(e, x3c=x3c, xkf=xkf, xvf=xvf):
                        for c in range(NFC):
                            e.tensor_scalar(xkf[:, 128 * c:128 * (c + 1)],
                                            x3c[:, 128 * c:128 * (c + 1)],
                                            mk_s[:, c:c + 1], 0.0,
                                            op0=OP.mult, op1=OP.add)
                        last = None
                        for c in range(NFC):
                            last = e.tensor_scalar(xvf[:, 128 * c:128 * (c + 1)],
                                                   x3c[:, 128 * c:128 * (c + 1)],
                                                   mv_s[:, c:c + 1], 0.0,
                                                   op0=OP.mult, op1=OP.add)
                        return last
                    vop(mix0, force=[bx3c], reads=[b_mk, b_mv],
                        writes=[bxkf, bxvf])
                else:
                    vop((lambda x3c=x3c, x3p=x3p, tp1=tp1: lambda e: e.tensor_tensor(
                        tp1, x3c, x3p, op=OP.subtract))(),
                        force=[bx3c, bx3p], writes=[btp1])
                    def mixs(e, tp1=tp1, x3p=x3p, xkf=xkf, xvf=xvf):
                        for c in range(NFC):
                            e.scalar_tensor_tensor(xkf[:, 128 * c:128 * (c + 1)],
                                                   tp1[:, 128 * c:128 * (c + 1)],
                                                   mk_s[:, c:c + 1],
                                                   x3p[:, 128 * c:128 * (c + 1)],
                                                   op0=OP.mult, op1=OP.add)
                        last = None
                        for c in range(NFC):
                            last = e.scalar_tensor_tensor(
                                xvf[:, 128 * c:128 * (c + 1)],
                                tp1[:, 128 * c:128 * (c + 1)],
                                mv_s[:, c:c + 1],
                                x3p[:, 128 * c:128 * (c + 1)],
                                op0=OP.mult, op1=OP.add)
                        return last
                    vop(mixs, force=[btp1, bx3p], reads=[b_mk, b_mv],
                        writes=[bxkf, bxvf])
                for (wdram, xsrc, bxsrc, dstt, bdstt, bank) in (
                        (d_wkb, xkf, bxkf, kfm, bkfm, 2),
                        (d_wvb, xvf, bxvf, vfm, bvfm, 3)):
                    for ko in range(NFC):
                        ws, bws = kv_sl[ko % 2]
                        dma("sync", ws.rearrange("p (f k) -> p f k", k=128),
                            wdram.rearrange("(f p) k -> f p k", p=128)
                            [:, :, 128 * ko:128 * (ko + 1)]
                            .rearrange("f p k -> p f k"),
                            writes=[bws])
                        for fi in range(NFC):
                            top((lambda fi=fi, ko=ko, ws=ws, xsrc=xsrc,
                                 bank=bank: lambda e: e.matmul(
                                     psum[bank][:, 0:128],
                                     ws[:, 128 * fi:128 * (fi + 1)],
                                     xsrc[:, 128 * fi:128 * (fi + 1)],
                                     start=(fi == 0), stop=(fi == NFC - 1)))(),
                                reads=[bxsrc, bws], writes=[b_ps[bank]])
                        vop((lambda ko=ko, dstt=dstt, bank=bank: lambda e: e.tensor_copy(
                            dstt[:, 128 * ko:128 * (ko + 1)],
                            psum[bank][:, 0:128]))(),
                            reads=[b_ps[bank]], writes=[bdstt])
                if s == 0:
                    vop((lambda ast=ast, vfm=vfm: lambda e: e.tensor_copy(ast, vfm))(),
                        force=[bvfm], writes=[bast])
                    vop((lambda bst=bst: lambda e: e.memset(bst, 1.0))(),
                        writes=[bbst])
                    vop((lambda pst=pst, kfm=kfm: lambda e: e.tensor_copy(pst, kfm))(),
                        force=[bkfm], writes=[bpst])
                else:
                    def pwf(e, pst=pst, tp1=tp1):
                        last = None
                        for c in range(NFC):
                            last = e.tensor_scalar(tp1[:, 128 * c:128 * (c + 1)],
                                                   pst[:, 128 * c:128 * (c + 1)],
                                                   td_s[:, c:c + 1], 0.0,
                                                   op0=OP.add, op1=OP.add)
                        return last
                    vop(pwf, force=[bpst], reads=[b_td], writes=[btp1])
                    vop((lambda tp1=tp1, kfm=kfm, tp2=tp2: lambda e: e.tensor_tensor(
                        tp2, tp1, kfm, op=OP.max))(),
                        force=[btp1, bkfm], writes=[btp2])
                    vop((lambda tp1=tp1, tp2=tp2: lambda e: e.tensor_tensor(
                        tp1, tp1, tp2, op=OP.subtract))(),
                        force=[btp1, btp2], writes=[btp1])
                    sop((lambda tp1=tp1: lambda e: e.activation(tp1, tp1, AF.Exp))(),
                        force=[btp1], writes=[btp1])
                    vop((lambda kfm=kfm, tp2=tp2, tp3=tp3: lambda e: e.tensor_tensor(
                        tp3, kfm, tp2, op=OP.subtract))(),
                        force=[bkfm, btp2], writes=[btp3])
                    sop((lambda tp3=tp3: lambda e: e.activation(tp3, tp3, AF.Exp))(),
                        force=[btp3], writes=[btp3])
                    vop((lambda ast=ast, tp1=tp1: lambda e: e.tensor_tensor(
                        ast, ast, tp1, op=OP.mult))(),
                        force=[bast, btp1], writes=[bast])
                    vop((lambda xkf=xkf, vfm=vfm, tp3=tp3: lambda e: e.tensor_tensor(
                        xkf, vfm, tp3, op=OP.mult))(),
                        force=[bvfm, btp3], writes=[bxkf])
                    vop((lambda ast=ast, xkf=xkf: lambda e: e.tensor_tensor(
                        ast, ast, xkf, op=OP.add))(),
                        force=[bast, bxkf], writes=[bast])
                    vop((lambda bst=bst, tp1=tp1: lambda e: e.tensor_tensor(
                        bst, bst, tp1, op=OP.mult))(),
                        force=[bbst, btp1], writes=[bbst])
                    vop((lambda bst=bst, tp3=tp3: lambda e: e.tensor_tensor(
                        bst, bst, tp3, op=OP.add))(),
                        force=[bbst, btp3], writes=[bbst])
                    vop((lambda pst=pst, tp2=tp2: lambda e: e.tensor_copy(pst, tp2))(),
                        force=[btp2], writes=[bpst])
                def kkf(e, kfm=kfm, tp1=tp1):
                    last = None
                    for c in range(NFC):
                        last = e.tensor_scalar(tp1[:, 128 * c:128 * (c + 1)],
                                               kfm[:, 128 * c:128 * (c + 1)],
                                               tdtf_s[:, c:c + 1], 0.0,
                                               op0=OP.add, op1=OP.add)
                    return last
                vop(kkf, force=[bkfm, bpst], reads=[b_tdtf], writes=[btp1])
                vop((lambda pst=pst, tp1=tp1, tp2=tp2: lambda e: e.tensor_tensor(
                    tp2, pst, tp1, op=OP.max))(),
                    force=[bpst, btp1], writes=[btp2])
                vop((lambda pst=pst, tp2=tp2, tp3=tp3: lambda e: e.tensor_tensor(
                    tp3, pst, tp2, op=OP.subtract))(),
                    force=[bpst, btp2], writes=[btp3])
                sop((lambda tp3=tp3: lambda e: e.activation(tp3, tp3, AF.Exp))(),
                    force=[btp3], writes=[btp3])
                vop((lambda tp1=tp1, tp2=tp2: lambda e: e.tensor_tensor(
                    tp1, tp1, tp2, op=OP.subtract))(),
                    force=[btp1, btp2], writes=[btp1])
                sop((lambda tp1=tp1: lambda e: e.activation(tp1, tp1, AF.Exp))(),
                    force=[btp1], writes=[btp1])
                vop((lambda ast=ast, tp3=tp3, xkf=xkf: lambda e: e.tensor_tensor(
                    xkf, ast, tp3, op=OP.mult))(),
                    force=[bast, btp3], writes=[bxkf])
                vop((lambda vfm=vfm, tp1=tp1, xvf=xvf: lambda e: e.tensor_tensor(
                    xvf, vfm, tp1, op=OP.mult))(),
                    force=[bvfm, btp1], writes=[bxvf])
                vop((lambda xkf=xkf, xvf=xvf: lambda e: e.tensor_tensor(
                    xkf, xkf, xvf, op=OP.add))(),
                    force=[bxkf, bxvf], writes=[bxkf])
                vop((lambda bst=bst, tp3=tp3, tp2=tp2: lambda e: e.tensor_tensor(
                    tp2, bst, tp3, op=OP.mult))(),
                    force=[bbst, btp3], writes=[btp2])
                vop((lambda tp2=tp2, tp1=tp1: lambda e: e.tensor_tensor(
                    tp2, tp2, tp1, op=OP.add))(),
                    force=[btp2, btp1], writes=[btp2])
                vop((lambda tp2=tp2: lambda e: e.reciprocal(tp2, tp2))(),
                    force=[btp2], writes=[btp2])
                vop((lambda xkf=xkf, tp2=tp2: lambda e: e.tensor_tensor(
                    xkf, xkf, tp2, op=OP.mult))(),
                    force=[bxkf, btp2], writes=[bxkf])
                for half in range(2):
                    for si in range(2):
                        dma("sync", wo_sl.rearrange("p (f k) -> p f k", k=512),
                            d_wob.rearrange("(f p) k -> f p k", p=128)
                            [4 * si:4 * (si + 1), :, 512 * half:512 * (half + 1)]
                            .rearrange("f p k -> p f k"),
                            writes=[bwo_sl])
                        for fj in range(4):
                            fi = 4 * si + fj
                            top((lambda fi=fi, fj=fj, half=half, xkf=xkf:
                                 lambda e: e.matmul(
                                     psum[4 + half][:, :],
                                     xkf[:, 128 * fi:128 * (fi + 1)],
                                     wo_sl[:, 512 * fj:512 * (fj + 1)],
                                     start=(fi == 0), stop=(fi == NFC - 1)))(),
                                reads=[bxkf, bwo_sl], writes=[b_ps[4 + half]])
                for half in range(2):
                    vop((lambda i=i, half=half: lambda e: e.tensor_tensor(
                        tile_x(i)[:, 512 * half:512 * (half + 1)],
                        tile_x(i)[:, 512 * half:512 * (half + 1)],
                        psum[4 + half][:, :], op=OP.add))(),
                        reads=[b_ps[4 + half]], writes=[b_xt[i]])

    if n_sub >= 4:
        emit_ffn("f2")

    sch.phase_barrier()
    d_y4 = d_y.rearrange("(n p) h -> n p h", p=128)
    stores = []
    for i in range(NT):
        stores.append((lambda i=i: lambda e: e.dma_start(
            out=d_y4[i, :, :], in_=tile_x(i)))())
    t_st = sch.add("sync", stores, reads=b_xt, is_dma=True)
    sch.add("gpsimd", lambda e: e.engine_nop(), after=[t_st])

    sch.build(nc, stack)
    return nc


_NC_CACHE = {}


def _get_nc(n_sub):
    if n_sub not in _NC_CACHE:
        _NC_CACHE[n_sub] = build_nc(n_sub)
    return _NC_CACHE[n_sub]


def kernel(**inputs):
    x = np.ascontiguousarray(np.asarray(inputs["L0_attn_a_input_tensor"], dtype=np.float32))
    Wq = np.asarray(inputs["L0_attn_a_Wq"], dtype=np.float32)
    Wk = np.asarray(inputs["L0_attn_a_Wk"], dtype=np.float32)
    Wv = np.asarray(inputs["L0_attn_a_Wv"], dtype=np.float32)
    Wd = np.asarray(inputs["L0_attn_a_Wd"], dtype=np.float32)
    bq = np.asarray(inputs["L0_attn_a_bq"], dtype=np.float32)
    bk = np.asarray(inputs["L0_attn_a_bk"], dtype=np.float32)
    bv = np.asarray(inputs["L0_attn_a_bv"], dtype=np.float32)
    bd = np.asarray(inputs["L0_attn_a_bd"], dtype=np.float32)
    ga = np.asarray(inputs["L0_attn_a_gamma"], dtype=np.float32)
    be = np.asarray(inputs["L0_attn_a_beta"], dtype=np.float32)
    tmrkv = np.asarray(inputs["L0_attn_b_tmrkv"], dtype=np.float32)
    wkb = np.asarray(inputs["L0_attn_b_Wk"], dtype=np.float32)
    wvb = np.asarray(inputs["L0_attn_b_Wv"], dtype=np.float32)
    wob = np.asarray(inputs["L0_attn_b_Wout"], dtype=np.float32)
    td = np.asarray(inputs["L0_attn_b_time_decay"], dtype=np.float32)
    tf = np.asarray(inputs["L0_attn_b_time_first"], dtype=np.float32)
    wg = np.asarray(inputs["L0_ffn_w_gate"], dtype=np.float32)
    wu = np.asarray(inputs["L0_ffn_w_up"], dtype=np.float32)
    wd = np.asarray(inputs["L0_ffn_w_down"], dtype=np.float32)

    def sliver(v):
        return np.ascontiguousarray(v.reshape(NFC, 128).T)

    def bcast(v):
        return np.ascontiguousarray(np.broadcast_to(v, (128, H)))

    common = {
        "ident": np.eye(128, dtype=np.float32),
        "wqT": np.ascontiguousarray(Wq.T), "wkT": np.ascontiguousarray(Wk.T),
        "wvT": np.ascontiguousarray(Wv.T), "wdT": np.ascontiguousarray(Wd.T),
        "bq": sliver(bq), "bk_bc": bcast(bk), "bv_bc": bcast(bv),
        "bd_bc": bcast(bd), "ga_bc": bcast(ga), "be_bc": bcast(be),
        "wkb": np.ascontiguousarray(wkb), "wvb": np.ascontiguousarray(wvb),
        "wob": np.ascontiguousarray(wob),
        "mk": sliver(tmrkv[H:2 * H]), "mv": sliver(tmrkv[2 * H:]),
        "td": sliver(td), "tf": sliver(tf),
        "wg": np.ascontiguousarray(wg), "wu": np.ascontiguousarray(wu),
        "wd": np.ascontiguousarray(wd),
    }
    in_maps = []
    for c in range(NCORES):
        m = dict(common)
        m["x"] = np.ascontiguousarray(x[:, c * SC:(c + 1) * SC, :].reshape(T, H))
        in_maps.append(m)

    nc = _get_nc(_N_SUB)
    res = run_bass_kernel_spmd(nc, in_maps, core_ids=list(range(NCORES)))
    out = np.empty((B, S, H), dtype=np.float32)
    for c in range(NCORES):
        out[:, c * SC:(c + 1) * SC, :] = res.results[c]["y"].reshape(B, SC, H)
    return out
